# version 07
# baseline (speedup 1.0000x reference)
import numpy as np
from contextlib import ExitStack

import concourse.bass as bass
import concourse.mybir as mybir
from concourse.bass_utils import run_bass_kernel_spmd

F32 = mybir.dt.float32
BF16 = mybir.dt.bfloat16
FP8 = mybir.dt.float8e4
AF = mybir.ActivationFunctionType
ALU = mybir.AluOpType
AX = mybir.AxisListType

MAGIC = 12582912.0
S, DM, HID, DH = 4096, 1024, 4096, 64
NCORE, OWN = 8, 2048
DR = OWN + 2
EPS = 1e-5
A_CH, N_ACH = 512, 8
QB, N_QB = 256, 16
D_BLK, N_DBLK = 256, 8
HB_SZ = [256] * 7 + [258]
N_STAT = 17



def _rope_tables():
    inv_freq = (1.0 / (np.float32(10000.0) **
                       (np.arange(0, DH, 2, dtype=np.float32) / np.float32(DH))))
    t = np.arange(S, dtype=np.float32)
    freqs = t[:, None] * inv_freq[None, :]
    cos = np.cos(freqs.astype(np.float64)).astype(np.float32)
    sin = np.sin(freqs.astype(np.float64)).astype(np.float32)
    return (np.ascontiguousarray(np.tile(cos.T, (4, 1))),
            np.ascontiguousarray(np.tile(sin.T, (4, 1))))


def _stage_inputs(x, w_qkv, w_out, w1, conv_w, conv_b, w2):
    cos4, sin4 = _rope_tables()
    eye = np.eye(128, dtype=np.float32)
    w1t = np.ascontiguousarray(w1.T)
    w2t = np.ascontiguousarray(w2.T)
    cwm = np.ascontiguousarray(conv_w.reshape(32, 128, 3).transpose(1, 0, 2)
                               .reshape(128, 96))
    cbm = np.ascontiguousarray(conv_b.reshape(32, 128).T)
    in_maps = []
    for core in range(NCORE):
        b, hh = core // 2, core % 2
        heads = [hh * 8 + j for j in range(8)]
        qcols = []
        for g in range(2):
            hs = heads[4 * g:4 * g + 4]
            for h in hs:
                qcols.extend(range(h * DH, h * DH + 32))
            for h in hs:
                qcols.extend(range(h * DH + 32, h * DH + 64))
        kcols = [c + DM for c in qcols]
        vcols = []
        for h in heads:
            vcols.extend(range(2 * DM + h * DH, 2 * DM + (h + 1) * DH))
        own0 = hh * OWN
        x0s = np.zeros((DR, DM), dtype=np.float32)
        lo, hi = own0 - 1, own0 + OWN + 1
        clo, chi = max(lo, 0), min(hi, S)
        x0s[clo - lo:chi - lo] = x[b, clo:chi]
        in_maps.append({
            "xfull": np.ascontiguousarray(x[b]),
            "x0s": x0s,
            "wq": np.ascontiguousarray(w_qkv[:, qcols]),
            "wk": np.ascontiguousarray(w_qkv[:, kcols]),
            "wv": np.ascontiguousarray(w_qkv[:, vcols]),
            "wo": np.ascontiguousarray(w_out[hh * 512:(hh + 1) * 512, :]),
            "w1t": w1t, "w2t": w2t, "cwm": cwm, "cbm": cbm,
            "cost": cos4, "sint": sin4, "eyef": eye,
        })
    return in_maps



class Sem:
    def __init__(self, stk, nc, name):
        self.h = stk.enter_context(nc.semaphore(name))
        self.n = 0

    def inc(self, instr, k=1):
        instr.then_inc(self.h, k)
        self.n += k
        return instr


def build_program(taps=(), phases="ABD"):
    nc = bass.Bass("TRN2", target_bir_lowering=False, enable_partition_id=False)
    taps = set(taps)
    tapd = {}

    d_x = nc.dram_tensor("xfull", [S, DM], F32, kind="ExternalInput").ap()
    d_x0s = nc.dram_tensor("x0s", [DR, DM], F32, kind="ExternalInput").ap()
    d_wq = nc.dram_tensor("wq", [DM, 512], F32, kind="ExternalInput").ap()
    d_wk = nc.dram_tensor("wk", [DM, 512], F32, kind="ExternalInput").ap()
    d_wv = nc.dram_tensor("wv", [DM, 512], F32, kind="ExternalInput").ap()
    d_wo = nc.dram_tensor("wo", [512, DM], F32, kind="ExternalInput").ap()
    d_w1t = nc.dram_tensor("w1t", [DM, HID], F32, kind="ExternalInput").ap()
    d_w2t = nc.dram_tensor("w2t", [HID, DM], F32, kind="ExternalInput").ap()
    d_cwm = nc.dram_tensor("cwm", [128, 96], F32, kind="ExternalInput").ap()
    d_cbm = nc.dram_tensor("cbm", [128, 32], F32, kind="ExternalInput").ap()
    d_cos = nc.dram_tensor("cost", [128, S], F32, kind="ExternalInput").ap()
    d_sin = nc.dram_tensor("sint", [128, S], F32, kind="ExternalInput").ap()
    d_eye = nc.dram_tensor("eyef", [128, 128], F32, kind="ExternalInput").ap()
    d_out = nc.dram_tensor("out", [OWN, DM], F32, kind="ExternalOutput").ap()
    d_ccin = nc.dram_tensor("cc_in", [2 * DR, DM], F32)
    d_ccout = nc.dram_tensor("cc_out", [DR, DM], F32)
    d_w1s = nc.dram_tensor("w1s", [1, 128], F32)
    d_w1v = nc.dram_tensor("w1v", [1, 2], F32)
    d_w2s = nc.dram_tensor("w2s", [1, 128], F32)
    d_w2v = nc.dram_tensor("w2v", [1, 2], F32)

    def tapt(name, shape, dtype=F32):
        if name in taps:
            tapd[name] = nc.dram_tensor("tap_" + name, list(shape), dtype,
                                        kind="ExternalOutput").ap()
            return tapd[name]
        return None

    stk = ExitStack()
    with stk:
        L = dict(side="left")
        R = dict(side="right")
        eyeb = stk.enter_context(nc.sbuf_tensor("eyeb", [128, 128], BF16, **L))
        eyef = stk.enter_context(nc.sbuf_tensor("eyefs", [128, 128], F32, **L))
        onesf = stk.enter_context(nc.sbuf_tensor("onesf", [128, 128], F32, **L))
        onesb = stk.enter_context(nc.sbuf_tensor("onesb", [128, 1], BF16, **L))
        zerob = stk.enter_context(nc.sbuf_tensor("zerob", [1, 512], BF16, **L))
        zerof = stk.enter_context(nc.sbuf_tensor("zerof", [1, DM], F32, **L))
        epsap = stk.enter_context(nc.sbuf_tensor("epsap", [128, 1], F32, **L))
        zeroap = stk.enter_context(nc.sbuf_tensor("zeroap", [128, 1], F32, **L))
        cwp = stk.enter_context(nc.sbuf_tensor("cwp", [128, 96], F32, **L))
        cwq = stk.enter_context(nc.sbuf_tensor("cwq", [128, 96], F32, **L))
        cbp = stk.enter_context(nc.sbuf_tensor("cbp", [128, 32], F32, **L))
        m_arr = stk.enter_context(nc.sbuf_tensor("m_arr", [128, N_STAT], F32, **L))
        var_arr = stk.enter_context(nc.sbuf_tensor("var_arr", [128, N_STAT], F32, **L))
        am_arr = stk.enter_context(nc.sbuf_tensor("am_arr", [128, N_STAT], F32, **L))
        rsx_arr = stk.enter_context(nc.sbuf_tensor("rsx_arr", [128, N_STAT], F32, **L))
        isx_arr = stk.enter_context(nc.sbuf_tensor("isx_arr", [128, N_STAT], F32, **L))
        ds17a = stk.enter_context(nc.sbuf_tensor("ds17a", [128, N_STAT], F32, **L))
        ds17b = stk.enter_context(nc.sbuf_tensor("ds17b", [128, N_STAT], F32, **L))
        ds17c = stk.enter_context(nc.sbuf_tensor("ds17c", [128, N_STAT], F32, **L))
        s1bc = stk.enter_context(nc.sbuf_tensor("s1bc", [128, 1], F32, **L))
        s2bc = stk.enter_context(nc.sbuf_tensor("s2bc", [128, 1], F32, **L))
        r127sw2 = stk.enter_context(nc.sbuf_tensor("r127sw2", [128, 1], F32, **L))
        scal = stk.enter_context(nc.sbuf_tensor("scal", [128, 8], F32, **L))

        sd = Sem(stk, nc, "sd")
        sT = Sem(stk, nc, "sT")
        sV = Sem(stk, nc, "sV")
        sA = Sem(stk, nc, "sA")
        sG = Sem(stk, nc, "sG")
        sE = Sem(stk, nc, "sE")
        sM = Sem(stk, nc, "sM")
        sAV = Sem(stk, nc, "sAV")
        sZR = Sem(stk, nc, "sZR")
        sBC = Sem(stk, nc, "sBC")
        sOC = Sem(stk, nc, "sOC")
        sOM = Sem(stk, nc, "sOM")
        sWO = Sem(stk, nc, "sWO")
        sAO = Sem(stk, nc, "sAO")
        scc = Sem(stk, nc, "scc")

        with nc.Block() as blk:
            @blk.sync
            def _(q):
                sd.inc(q.dma_start(out=eyef[:, :], in_=d_eye), 16)
                sd.inc(q.dma_start(out=cwp[:, :], in_=d_cwm), 16)
                sd.inc(q.dma_start(out=cbp[:, :], in_=d_cbm), 16)

            @blk.gpsimd
            def _(g):
                g.memset(onesf[:, :], 1.0)
                g.memset(onesb[:, :], 1.0)
                g.memset(zerob[:, :], 0.0)
                g.memset(zerof[:, :], 0.0)
                g.memset(epsap[:, :], EPS)
                g.memset(zeroap[:, :], 0.0)

            @blk.vector
            def _(v):
                v.wait_ge(sd.h, sd.n)
                v.tensor_copy(eyeb[:, :], eyef[:, :])

        with ExitStack() as st_ab:
            qx1 = st_ab.enter_context(nc.sbuf_tensor("qx1", [128, 2, S], BF16, **R))
            qx2 = st_ab.enter_context(nc.sbuf_tensor("qx2", [128, 2, S], BF16, **R))
            kx1 = st_ab.enter_context(nc.sbuf_tensor("kx1", [128, 2, S], BF16, **R))
            kx2 = st_ab.enter_context(nc.sbuf_tensor("kx2", [128, 2, S], BF16, **R))
            v_tm = st_ab.enter_context(nc.sbuf_tensor("v_tm", [128, 32, 512], BF16, **R))
            wo_sb = st_ab.enter_context(nc.sbuf_tensor("wo_sb", [128, 4, DM], BF16, **R))

            with ExitStack() as st_a:
                wq_sb = st_a.enter_context(nc.sbuf_tensor("wq_sb", [128, 8, 512], BF16, **R))
                wk_sb = st_a.enter_context(nc.sbuf_tensor("wk_sb", [128, 8, 512], BF16, **R))
                wv_sb = st_a.enter_context(nc.sbuf_tensor("wv_sb", [128, 8, 512], BF16, **R))
                wstg = st_a.enter_context(nc.sbuf_tensor("wstg", [128, 8, 512], F32, **R))
                xa = st_a.enter_context(nc.sbuf_tensor("xa", [128, 4, DM], F32, **R))
                xn_tm = st_a.enter_context(nc.sbuf_tensor("xn_tm", [128, 4, DM], BF16, **R))
                xnf = st_a.enter_context(nc.sbuf_tensor("xnf", [128, 8, A_CH], BF16, **R))
                qev = st_a.enter_context(nc.sbuf_tensor("qev", [128, 4, A_CH], BF16, **R))
                kev = st_a.enter_context(nc.sbuf_tensor("kev", [128, 4, A_CH], BF16, **R))
                cosl = st_a.enter_context(nc.sbuf_tensor("cosl", [128, A_CH], BF16, **R))
                sinl = st_a.enter_context(nc.sbuf_tensor("sinl", [128, A_CH], BF16, **R))
                tstg = st_a.enter_context(nc.sbuf_tensor("tstg", [128, 2, A_CH], F32, **R))
                rt1 = st_a.enter_context(nc.sbuf_tensor("rt1", [128, A_CH], BF16, **R))
                rt2 = st_a.enter_context(nc.sbuf_tensor("rt2", [128, A_CH], BF16, **R))
                sqscr = st_a.enter_context(nc.sbuf_tensor("sqscr", [128, DM], BF16, **R))
                stt = st_a.enter_context(nc.sbuf_tensor("stt", [128, 32], F32, **R))
                p_t = st_a.enter_context(nc.psum_tensor("p_t", [128, 2, 1024], BF16))
                p_qkv = st_a.enter_context(nc.psum_tensor("p_qkv", [128, 4, 512], F32))

                with nc.Block() as blk:
                    for dsrc, wdst in ((d_wq, wq_sb), (d_wk, wk_sb), (d_wv, wv_sb)):
                        @blk.sync
                        def _(q, dsrc=dsrc):
                            q.wait_ge(sV.h, sV.n)
                            sd.inc(q.dma_start(
                                out=wstg[:, :, :],
                                in_=dsrc.rearrange("(dc p) f -> p dc f", p=128)), 16)
                        wdma = sd.n

                        @blk.vector
                        def _(v, wdst=wdst, wdma=wdma):
                            v.wait_ge(sd.h, wdma)
                            sV.inc(v.tensor_copy(wdst[:, :, :], wstg[:, :, :]))

                    @blk.scalar
                    def _(s):
                        s.preload_act_table(AF.Sqrt)

                with nc.Block() as blk:
                    v_chunk_done = 0
                    a_chunk_done = 0
                    ev_last = 0
                    for c in range(N_ACH):
                        T0 = c * A_CH
                        _vcd, _acd = v_chunk_done, a_chunk_done

                        @blk.sync
                        def _(q, T0=T0, _vcd=_vcd, _acd=_acd):
                            q.wait_ge(sV.h, _vcd)
                            q.wait_ge(sA.h, _acd)
                            sd.inc(q.dma_start(
                                out=xa[:, :, :],
                                in_=d_x[T0:T0 + A_CH, :].rearrange(
                                    "(tt p) d -> p tt d", p=128)), 16)
                            sd.inc(q.dma_start(out=tstg[:, 0, :],
                                               in_=d_cos[:, T0:T0 + A_CH]), 16)
                            sd.inc(q.dma_start(out=tstg[:, 1, :],
                                               in_=d_sin[:, T0:T0 + A_CH]), 16)
                        dma_x = sd.n

                        @blk.vector
                        def _(v, dma_x=dma_x):
                            v.wait_ge(sd.h, dma_x)
                            v.tensor_copy(cosl[:, :], tstg[:, 0, :])
                            sV.inc(v.tensor_copy(sinl[:, :], tstg[:, 1, :]))

                        for tt in range(4):
                            @blk.scalar
                            def _(s, tt=tt, dma_x=dma_x):
                                s.wait_ge(sd.h, dma_x)
                                sA.inc(s.activation(sqscr[:, :], xa[:, tt, :],
                                                    AF.Square, accum_out=stt[:, 4 + tt:5 + tt]))
                            sq_done = sA.n

                            @blk.vector
                            def _(v, tt=tt, dma_x=dma_x, sq_done=sq_done):
                                v.wait_ge(sd.h, dma_x)
                                def VF(instr):
                                    sV.inc(instr)
                                    v.wait_ge(sV.h, sV.n)
                                VF(v.reduce_sum(stt[:, tt:tt + 1], xa[:, tt, :], AX.X))
                                v.wait_ge(sA.h, sq_done)
                                VF(v.tensor_scalar_mul(stt[:, 24 + tt:25 + tt],
                                                       stt[:, tt:tt + 1], 1.0 / DM))
                                VF(v.tensor_scalar_mul(stt[:, 8 + tt:9 + tt],
                                                       stt[:, 4 + tt:5 + tt], 1.0 / DM))
                                VF(v.tensor_mul(stt[:, 12 + tt:13 + tt],
                                                stt[:, 24 + tt:25 + tt],
                                                stt[:, 24 + tt:25 + tt]))
                                VF(v.tensor_sub(stt[:, 16 + tt:17 + tt],
                                                stt[:, 8 + tt:9 + tt],
                                                stt[:, 12 + tt:13 + tt]))
                                if tt == 3:
                                    VF(v.tensor_single_scalar(stt[:, 20:24], stt[:, 16:20],
                                                              epsap[:, 0:1], ALU.add))
                        veps_done = sV.n

                        @blk.scalar
                        def _(s, veps_done=veps_done):
                            s.wait_ge(sV.h, veps_done)
                            sA.inc(s.activation(stt[:, 8:12], stt[:, 20:24], AF.Sqrt,
                                                bias=zeroap[:, 0:1]))
                        sqrt_done = sA.n

                        @blk.vector
                        def _(v, sqrt_done=sqrt_done):
                            v.wait_ge(sA.h, sqrt_done)
                            def VF(instr):
                                sV.inc(instr)
                                v.wait_ge(sV.h, sV.n)
                            VF(v.reciprocal(stt[:, 12:16], stt[:, 8:12]))
                            VF(v.tensor_mul(stt[:, 16:20], stt[:, 20:24], stt[:, 12:16]))
                            VF(v.tensor_add(stt[:, 12:16], stt[:, 16:20], stt[:, 8:12]))
                            VF(v.tensor_scalar_mul(stt[:, 16:20], stt[:, 12:16], 0.5))
                            VF(v.reciprocal(stt[:, 28:32], stt[:, 16:20]))
                            for tt in range(4):
                                instr = v.tensor_scalar(xn_tm[:, tt, :], xa[:, tt, :],
                                                        stt[:, 24 + tt:25 + tt],
                                                        stt[:, 28 + tt:29 + tt],
                                                        ALU.subtract, ALU.mult)
                                if tt == 3:
                                    sV.inc(instr)
                        ln_done = sV.n

                        tcopy0 = sV.n
                        for j in range(32):
                            tt, dc = j // 8, j % 8

                            @blk.tensor
                            def _(t, tt=tt, dc=dc, j=j, ln_done=ln_done, tcopy0=tcopy0):
                                t.wait_ge(sV.h, ln_done if j < 2 else tcopy0 + j - 1)
                                sT.inc(t.transpose(p_t[:, j % 2, 0:128],
                                                   xn_tm[:, tt, dc * 128:(dc + 1) * 128],
                                                   eyeb[:, :]))
                            tr_done = sT.n

                            @blk.vector
                            def _(v, tt=tt, dc=dc, j=j, tr_done=tr_done):
                                v.wait_ge(sT.h, tr_done)
                                sV.inc(v.tensor_copy(xnf[:, dc, tt * 128:(tt + 1) * 128],
                                                     p_t[:, j % 2, 0:128]))
                        xnf_done = sV.n

                        for wsb, evt, x1t, x2t, qscale in (
                                (wq_sb, qev, qx1, qx2, 0.125),
                                (wk_sb, kev, kx1, kx2, 1.0)):
                            _evl = ev_last

                            @blk.tensor
                            def _(t, wsb=wsb, xnf_done=xnf_done, _evl=_evl):
                                t.wait_ge(sV.h, xnf_done)
                                t.wait_ge(sA.h, _evl)
                                mm = None
                                for fc in range(4):
                                    for dc in range(8):
                                        mm = t.matmul(p_qkv[:, fc, :],
                                                      wsb[:, dc, fc * 128:(fc + 1) * 128],
                                                      xnf[:, dc, :],
                                                      start=(dc == 0), stop=(dc == 7))
                                sT.inc(mm)
                            mm_done = sT.n

                            @blk.scalar
                            def _(s, evt=evt, qscale=qscale, mm_done=mm_done):
                                s.wait_ge(sT.h, mm_done)
                                for fc in range(4):
                                    sA.inc(s.activation(evt[:, fc, :], p_qkv[:, fc, :],
                                                        AF.Copy, scale=qscale))
                            ev_last = sA.n
                            _evd = sA.n

                            @blk.vector
                            def _(v, evt=evt, x1t=x1t, x2t=x2t, T0=T0, _evd=_evd):
                                v.wait_ge(sA.h, _evd)
                                for g in range(2):
                                    v.tensor_mul(rt1[:, :], evt[:, 2 * g, :], cosl[:, :])
                                    v.tensor_mul(rt2[:, :], evt[:, 2 * g + 1, :], sinl[:, :])
                                    v.tensor_sub(x1t[:, g, T0:T0 + A_CH],
                                                 rt1[:, :], rt2[:, :])
                                    v.tensor_mul(rt1[:, :], evt[:, 2 * g + 1, :], cosl[:, :])
                                    v.tensor_mul(rt2[:, :], evt[:, 2 * g, :], sinl[:, :])
                                    sV.inc(v.tensor_add(x2t[:, g, T0:T0 + A_CH],
                                                        rt1[:, :], rt2[:, :]))

                        _evl = ev_last

                        @blk.tensor
                        def _(t, xnf_done=xnf_done, _evl=_evl):
                            t.wait_ge(sV.h, xnf_done)
                            t.wait_ge(sA.h, _evl)
                            mm = None
                            for tt in range(4):
                                for dc in range(8):
                                    mm = t.matmul(p_qkv[:, tt, :],
                                                  xnf[:, dc, tt * 128:(tt + 1) * 128],
                                                  wv_sb[:, dc, :],
                                                  start=(dc == 0), stop=(dc == 7))
                            sT.inc(mm)
                        vmm_done = sT.n

                        @blk.scalar
                        def _(s, c=c, vmm_done=vmm_done):
                            s.wait_ge(sT.h, vmm_done)
                            for tt in range(4):
                                sA.inc(s.activation(v_tm[:, 4 * c + tt, :],
                                                    p_qkv[:, tt, :], AF.Copy))
                        ev_last = sA.n
                        v_chunk_done = sV.n
                        a_chunk_done = sA.n

                def emit_bf16_tap(name, src_flat, cols):
                    dst = tapt(name, (128, cols))
                    if dst is None:
                        return
                    for off in range(0, cols, 4096):
                        w = min(4096, cols - off)
                        with nc.Block() as blk:
                            @blk.vector
                            def _(v, off=off, w=w):
                                v.tensor_copy(
                                    wstg.ap().rearrange("p a b -> p (a b)")[:, 0:w],
                                    src_flat[:, off:off + w])
                        with nc.Block() as blk:
                            @blk.sync
                            def _(q, off=off, w=w):
                                sd.inc(q.dma_start(
                                    out=dst[:, off:off + w],
                                    in_=wstg.ap().rearrange("p a b -> p (a b)")[:, 0:w]), 16)
                                q.wait_ge(sd.h, sd.n)

                emit_bf16_tap("xn7", xn_tm.ap().rearrange("p a b -> p (a b)"), 4 * DM)
                emit_bf16_tap("xnf7", xnf.ap().rearrange("p a b -> p (a b)"), 8 * A_CH)
                emit_bf16_tap("qx1", qx1.ap().rearrange("p g s -> p (g s)"), 2 * S)
                emit_bf16_tap("qx2", qx2.ap().rearrange("p g s -> p (g s)"), 2 * S)
                emit_bf16_tap("kx1", kx1.ap().rearrange("p g s -> p (g s)"), 2 * S)
                emit_bf16_tap("kx2", kx2.ap().rearrange("p g s -> p (g s)"), 2 * S)
                emit_bf16_tap("v_tm", v_tm.ap().rearrange("p a b -> p (a b)"), 32 * 512)

            DO_B = "B" in phases
            w1q = stk.enter_context(nc.sbuf_tensor("w1q", [128, 8, HID], FP8, side="left"))

            with ExitStack() as st_b:
                pT8 = st_b.enter_context(nc.sbuf_tensor("pT8", [128, 2, 2048], BF16, **R))
                zsb = st_b.enter_context(nc.sbuf_tensor("zsb", [64, 2, 512], F32, **R))
                rzsb = st_b.enter_context(nc.sbuf_tensor("rzsb", [64, 2, 512], F32, **R))
                of32 = st_b.enter_context(nc.sbuf_tensor("of32", [128, 4, 256], F32, **R))
                o_ep = st_b.enter_context(nc.sbuf_tensor("o_ep", [128, 4, 256], BF16, **R))
                aosb = st_b.enter_context(nc.sbuf_tensor("aosb", [128, 2, DM], F32, **R))
                wstgb = st_b.enter_context(nc.sbuf_tensor("wstgb", [128, 2, 2048], F32, **R))
                wqscr = st_b.enter_context(nc.sbuf_tensor("wqscr", [128, 2048], F32, **R))
                sr1 = st_b.enter_context(nc.sbuf_tensor("sr1", [1, 132], F32, **R))
                sttb = st_b.enter_context(nc.sbuf_tensor("sttb", [128, 8], F32, **R))
                p_big = st_b.enter_context(nc.psum_tensor("p_big", [128, 2048], F32))
                p_o = st_b.enter_context(nc.psum_tensor("p_o", [128, 1024], F32))
                p_z = st_b.enter_context(nc.psum_tensor("p_z", [128, 512], F32))
                p_rz = st_b.enter_context(nc.psum_tensor("p_rz", [128, 512], F32))

                def hcol(h):
                    return 512 * (h % 4) + 256 * (h // 4)

                with nc.Block() as blk:
                  if DO_B:
                    @blk.scalar
                    def _(s):
                        s.preload_act_table(AF.Exp)

                    @blk.sync
                    def _(q):
                        sd.inc(q.dma_start(
                            out=wstgb.ap().rearrange("p a b -> p (a b)").rearrange(
                                "p (pc d) -> p pc d", d=DM),
                            in_=d_wo.rearrange("(pc p) d -> p pc d", p=128)), 16)
                        sd.inc(q.dma_start(out=d_ccin[0:1, :], in_=zerof[0:1, :]), 16)
                        sd.inc(q.dma_start(out=d_ccin[2 * DR - 1:2 * DR, :],
                                           in_=zerof[0:1, :]), 16)
                    wo_dma = sd.n

                    @blk.vector
                    def _(v):
                        v.wait_ge(sd.h, wo_dma)
                        sV.inc(v.tensor_copy(
                            wo_sb[:, :, :],
                            wstgb.ap().rearrange("p a b -> p (a b)").rearrange(
                                "p (pc d) -> p pc d", d=DM)))

                with nc.Block() as blk:
                  if DO_B:
                    st_state = {
                        "w1_stage_free_v": sV.n,
                        "w1_stage_free_g": sG.n,
                        "prev_ao": 0,
                        "prev_oc": 0,
                        "prev_zr": 0,
                        "prev_om": 0,
                        "prev_avz": 0,
                        "prev_ccin": 0,
                        "pz_scrib_v": 0,
                    }
                    exp_at = {}
                    mask_at = {}
                    avz_at = {}

                    def emit_w1_mean(dc):
                        _vf = st_state["w1_stage_free_v"]
                        _gf = st_state["w1_stage_free_g"]

                        @blk.sync
                        def _(q, dc=dc, _vf=_vf):
                            q.wait_ge(sV.h, _vf)
                            sd.inc(q.dma_start(
                                out=wstgb[:, 0, :],
                                in_=d_w1t[128 * dc:128 * (dc + 1), 0:2048]), 16)
                            sd.inc(q.dma_start(
                                out=wstgb[:, 1, :],
                                in_=d_w1t[128 * dc:128 * (dc + 1), 2048:4096]), 16)
                        dmac = sd.n

                        @blk.vector
                        def _(v, dc=dc, dmac=dmac):
                            v.wait_ge(sd.h, dmac)
                            def VF(instr):
                                sV.inc(instr)
                                v.wait_ge(sV.h, sV.n)
                            VF(v.tensor_reduce(sttb[:, 0:1], wstgb[:, 0, :], AX.X,
                                               ALU.add, apply_absolute_value=True))
                            VF(v.tensor_reduce(sttb[:, 1:2], wstgb[:, 1, :], AX.X,
                                               ALU.add, apply_absolute_value=True))
                            VF(v.tensor_add(sttb[:, 2:3], sttb[:, 0:1], sttb[:, 1:2]))
                            cp = 3 + (dc % 2)
                            cq = 3 + ((dc + 1) % 2)
                            if dc == 0:
                                sV.inc(v.tensor_copy(sttb[:, cp:cp + 1], sttb[:, 2:3]))
                            else:
                                sV.inc(v.tensor_add(sttb[:, cp:cp + 1],
                                                    sttb[:, cq:cq + 1], sttb[:, 2:3]))
                            v.wait_ge(sV.h, sV.n)
                        st_state["w1_stage_free_v"] = sV.n

                    def emit_w1_scale():
                        prev = sV.n

                        @blk.sync
                        def _(q, prev=prev):
                            q.wait_ge(sV.h, prev)
                            sd.inc(q.dma_start(out=d_w1s.ap(), in_=sttb[:, 4:5]), 16)
                            q.wait_ge(sd.h, sd.n)
                            sd.inc(q.dma_start(out=sr1[0:1, 0:128], in_=d_w1s.ap()), 16)
                        rowd = sd.n

                        @blk.vector
                        def _(v, rowd=rowd):
                            v.wait_ge(sd.h, rowd)
                            def VF(instr):
                                sV.inc(instr)
                                v.wait_ge(sV.h, sV.n)
                            VF(v.reduce_sum(sr1[0:1, 128:129], sr1[0:1, 0:128], AX.X))
                            VF(v.tensor_scalar(sr1[0:1, 129:130], sr1[0:1, 128:129],
                                               1.0 / (DM * HID), 1e-5, ALU.mult, ALU.max))
                            VF(v.reciprocal(sr1[0:1, 130:131], sr1[0:1, 129:130]))
                        rcd = sV.n

                        @blk.sync
                        def _(q, rcd=rcd):
                            q.wait_ge(sV.h, rcd)
                            sd.inc(q.dma_start(out=d_w1v.ap(), in_=sr1[0:1, 129:131]), 16)
                            q.wait_ge(sd.h, sd.n)
                            sd.inc(q.dma_start(
                                out=s1bc[:, :],
                                in_=bass.AP(d_w1v.ap().tensor, 1, [[0, 128], [1, 1]])), 16)
                            sd.inc(q.dma_start(
                                out=scal[:, 0:1],
                                in_=bass.AP(d_w1v.ap().tensor, 0, [[0, 128], [1, 1]])), 16)
                        bcdma = sd.n

                        @blk.vector
                        def _(v, bcdma=bcdma):
                            v.wait_ge(sd.h, bcdma)
                            sV.inc(v.tensor_scalar_mul(cwq[:, :], cwp[:, :],
                                                       scal[:, 0:1]))
                            v.wait_ge(sV.h, sV.n)
                        st_state["w1_stage_free_v"] = sV.n

                    def emit_w1_quant(dc):
                        _vf = st_state["w1_stage_free_v"]
                        _gf = st_state["w1_stage_free_g"]

                        @blk.sync
                        def _(q, dc=dc, _vf=_vf):
                            q.wait_ge(sV.h, _vf)
                            sd.inc(q.dma_start(
                                out=wstgb[:, 0, :],
                                in_=d_w1t[128 * dc:128 * (dc + 1), 0:2048]), 16)
                            sd.inc(q.dma_start(
                                out=wstgb[:, 1, :],
                                in_=d_w1t[128 * dc:128 * (dc + 1), 2048:4096]), 16)
                        dmac = sd.n

                        for half in range(2):
                            @blk.vector
                            def _(v, dc=dc, dmac=dmac, half=half):
                                v.wait_ge(sd.h, dmac)
                                v.tensor_scalar(wqscr[:, :],
                                                wstgb[:, half, :],
                                                s1bc[:, 0:1], MAGIC,
                                                ALU.mult, ALU.add)
                                v.tensor_scalar(wstgb[:, half, :], wqscr[:, :],
                                                MAGIC, -1.0, ALU.subtract, ALU.max)
                                sV.inc(v.tensor_scalar_min(
                                    w1q[:, dc, 2048 * half:2048 * (half + 1)],
                                    wstgb[:, half, :], 1.0))
                        st_state["w1_stage_free_v"] = sV.n

                    w1_jobs = ([("m", dc) for dc in range(8)] + [("s", 0)] +
                               [("q", dc) for dc in range(8)])
                    w1_ji = [0]

                    def emit_w1_some(n_jobs):
                        for _ in range(n_jobs):
                            if w1_ji[0] >= len(w1_jobs):
                                return
                            kind, dc = w1_jobs[w1_ji[0]]
                            w1_ji[0] += 1
                            if kind == "m":
                                emit_w1_mean(dc)
                            elif kind == "s":
                                emit_w1_scale()
                            else:
                                emit_w1_quant(dc)

                    for qb in range(N_QB):
                        q0 = qb * QB
                        KT = 2 * qb + 2
                        _poc, _pzr = st_state["prev_oc"], st_state["prev_zr"]

                        @blk.tensor
                        def _(t, _poc=_poc, _pzr=_pzr):
                            t.wait_ge(sOC.h, _poc)
                            t.wait_ge(sZR.h, _pzr)
                            for half in range(2):
                                t.matmul(p_o[:, 512 * half:512 * half + 512],
                                         zerob[0:1, 0:128], zerob[0:1, 0:512],
                                         start=True, stop=True)
                            sT.inc(t.matmul(p_z[:, :], zerob[0:1, 0:128],
                                            zerob[0:1, 0:512], start=True, stop=True))

                        def emit_scores(kt):
                            _pao = st_state["prev_ao"]

                            @blk.tensor
                            def _(t, kt=kt, qb=qb, q0=q0, _pao=_pao):
                                k0 = kt * 128
                                if kt >= 1:
                                    t.wait_ge(sE.h, exp_at[(qb, kt - 1)])
                                else:
                                    t.wait_ge(sAO.h, _pao)
                                last = None
                                for g in range(2):
                                    for j in range(4):
                                        h = 4 * g + j
                                        sl = slice(32 * j, 32 * j + 32)
                                        co = hcol(h)
                                        t.matmul(p_big[:, co:co + 256],
                                                 kx1[sl, g, k0:k0 + 128],
                                                 qx1[sl, g, q0:q0 + QB],
                                                 start=True, stop=False,
                                                 tile_position=(32 * j, 0))
                                        last = t.matmul(p_big[:, co:co + 256],
                                                        kx2[sl, g, k0:k0 + 128],
                                                        qx2[sl, g, q0:q0 + QB],
                                                        start=False, stop=True,
                                                        tile_position=(32 * j, 0))
                                sT.inc(last)

                        def emit_exp(kt):
                            st_cnt = sT.n
                            _pavz = st_state["prev_avz"]

                            @blk.scalar
                            def _(s, kt=kt, qb=qb, st_cnt=st_cnt, _pavz=_pavz):
                                s.wait_ge(sT.h, st_cnt)
                                if kt >= 2:
                                    s.wait_ge(sAV.h, avz_at[(qb, kt - 2)])
                                elif _pavz:
                                    s.wait_ge(sAV.h, _pavz)
                                sE.inc(s.activation(pT8[:, kt % 2, :], p_big[:, :],
                                                    AF.Exp))
                            exp_at[(qb, kt)] = sE.n

                        def emit_mask(kt, d):
                            @blk.gpsimd
                            def _(g, kt=kt, d=d, qb=qb):
                                g.wait_ge(sE.h, exp_at[(qb, kt)])
                                sM.inc(g.affine_select(
                                    pT8.ap()[:, kt % 2, :].rearrange(
                                        "p (h j) -> p h j", j=256),
                                    pT8.ap()[:, kt % 2, :].rearrange(
                                        "p (h j) -> p h j", j=256),
                                    [[0, 8], [1, 256]],
                                    ALU.is_ge, 0.0,
                                    base=-128 * d,
                                    channel_multiplier=-1))
                            mask_at[(qb, kt)] = sM.n

                        def emit_avz(kt):
                            @blk.tensor
                            def _(t, kt=kt, qb=qb, KT=KT):
                                if kt >= KT - 2:
                                    t.wait_ge(sM.h, mask_at[(qb, kt)])
                                else:
                                    t.wait_ge(sE.h, exp_at[(qb, kt)])
                                lastkt = (kt == KT - 1)
                                buf = kt % 2
                                for p in range(4):
                                    for sub in range(2):
                                        h = 2 * p + sub
                                        t.matmul(
                                            p_o[64 * sub:64 * sub + 64,
                                                256 * p:256 * p + 256],
                                            v_tm[:, kt, 64 * h:64 * h + 64],
                                            pT8[:, buf, hcol(h):hcol(h) + 256],
                                            start=False, stop=lastkt,
                                            tile_position=(0, 64 * sub))
                                last = None
                                for gp in range(4):
                                    last = t.matmul(
                                        p_z[32 * gp:32 * gp + 1, :],
                                        onesb[:, 0:1],
                                        pT8[:, buf, 512 * gp:512 * gp + 512],
                                        start=False, stop=lastkt,
                                        tile_position=(0, 32 * gp))
                                sAV.inc(last)
                            avz_at[(qb, kt)] = sAV.n

                        for kt in range(KT):
                            emit_scores(kt)
                            emit_exp(kt)
                            if kt >= KT - 2:
                                emit_mask(kt, kt - (KT - 2))
                            if kt >= 1:
                                emit_avz(kt - 1)
                        emit_avz(KT - 1)
                        avz_all = sAV.n
                        st_state["prev_avz"] = avz_all

                        @blk.vector
                        def _(v, avz_all=avz_all):
                            v.wait_ge(sAV.h, avz_all)
                            v.tensor_copy(zsb[:, 0, :], p_z[0:64, :])
                            sV.inc(v.tensor_copy(zsb[:, 1, :], p_z[64:128, :]))
                            v.wait_ge(sV.h, sV.n)
                            v.reciprocal(rzsb[:, 0, :], zsb[:, 0, :])
                            sV.inc(v.reciprocal(rzsb[:, 1, :], zsb[:, 1, :]))
                            v.wait_ge(sV.h, sV.n)
                            sV.inc(v.tensor_copy(sttb[0:1, 0:1], rzsb[0:1, 1, 0:8][0:1, 0:1]))
                            v.wait_ge(sV.h, sV.n)
                            sZR.inc(v.tensor_copy(sttb[0:1, 1:2], rzsb[0:1, 0, 0:8][0:1, 0:1]))
                        zr_done = sZR.n
                        st_state["prev_zr"] = sZR.n

                        _pom0 = st_state["prev_om"]

                        @blk.scalar
                        def _(s, avz_all=avz_all, _pom0=_pom0):
                            s.wait_ge(sAV.h, avz_all)
                            s.wait_ge(sOM.h, _pom0)
                            for p in range(4):
                                sOC.inc(s.activation(of32[:, p, :],
                                                     p_o[:, 256 * p:256 * p + 256],
                                                     AF.Copy))
                        oc_done = sOC.n
                        st_state["prev_oc"] = sOC.n

                        for half in range(2):
                            _pom = st_state["prev_om"]

                            @blk.tensor
                            def _(t, half=half, zr_done=zr_done, _pom=_pom):
                                t.wait_ge(sZR.h, zr_done)
                                t.wait_ge(sOM.h, _pom)
                                last = None
                                for pp in range(2):
                                    p = 2 * half + pp
                                    for sub in range(2):
                                        h = 2 * p + sub
                                        j, g = h % 4, h // 4
                                        tile = j // 2
                                        pr = 32 * (j % 2)
                                        last = t.matmul(
                                            p_rz[64 * sub:64 * sub + 64,
                                                 256 * pp:256 * pp + 256],
                                            onesf[pr:pr + 1, 0:64],
                                            rzsb[pr:pr + 1, tile,
                                                 256 * g:256 * g + 256],
                                            start=True, stop=True,
                                            tile_position=(pr, 64 * sub))
                                sBC.inc(last)
                            bc_done = sBC.n

                            @blk.vector
                            def _(v, half=half, bc_done=bc_done, oc_done=oc_done):
                                v.wait_ge(sBC.h, bc_done)
                                v.wait_ge(sOC.h, oc_done)
                                for pp in range(2):
                                    p = 2 * half + pp
                                    sOM.inc(v.tensor_mul(
                                        o_ep[:, p, :], of32[:, p, :],
                                        p_rz[:, 256 * pp:256 * pp + 256]))
                            st_state["prev_om"] = sOM.n
                        om_done = sOM.n

                        @blk.tensor
                        def _(t, qb=qb, om_done=om_done, KT=KT):
                            t.wait_ge(sOM.h, om_done)
                            t.wait_ge(sE.h, exp_at[(qb, KT - 1)])
                            for tc in range(2):
                                for n in range(2):
                                    mm = None
                                    for p in range(4):
                                        mm = t.matmul(
                                            p_big[:, 1024 * tc + 512 * n:
                                                  1024 * tc + 512 * n + 512],
                                            o_ep[:, p, 128 * tc:128 * tc + 128],
                                            wo_sb[:, p, 512 * n:512 * n + 512],
                                            start=(p == 0), stop=(p == 3))
                                    sWO.inc(mm)
                        wo_done = sWO.n

                        _pcc = st_state["prev_ccin"]

                        @blk.vector
                        def _(v, wo_done=wo_done, _pcc=_pcc):
                            v.wait_ge(sWO.h, wo_done)
                            v.wait_ge(sd.h, _pcc)
                            for tc in range(2):
                                sAO.inc(v.tensor_copy(
                                    aosb[:, tc, :],
                                    p_big[:, 1024 * tc:1024 * tc + 1024]))
                        ao_done = sAO.n
                        st_state["prev_ao"] = ao_done

                        @blk.sync
                        def _(q, q0=q0, ao_done=ao_done):
                            q.wait_ge(sAO.h, ao_done)
                            for tc in range(2):
                                t0 = q0 + 128 * tc
                                if t0 <= OWN:
                                    n_r = min(128, OWN + 1 - t0)
                                    sd.inc(q.dma_start(
                                        out=d_ccin[t0 + 1:t0 + 1 + n_r, :],
                                        in_=aosb[0:n_r, tc, :]), 16)
                                lo = max(t0, OWN - 1)
                                if lo < t0 + 128:
                                    off = lo - t0
                                    n_r = 128 - off
                                    r0 = DR + lo - (OWN - 1)
                                    sd.inc(q.dma_start(
                                        out=d_ccin[r0:r0 + n_r, :],
                                        in_=aosb[off:off + n_r, tc, :]), 16)
                        st_state["prev_ccin"] = sd.n

                        emit_w1_some(2 if qb < 8 else 1)

                    if "o_last" in taps:
                        dst = tapt("o_last", (128, 4 * 256))
                        @blk.vector
                        def _(v):
                            v.tensor_copy(wstgb.ap().rearrange("p a b -> p (a b)")[:, 0:1024],
                                          o_ep.ap().rearrange("p a b -> p (a b)"))
                        @blk.sync
                        def _(q):
                            q.wait_ge(sV.h, sV.n)
                            sd.inc(q.dma_start(
                                out=dst,
                                in_=wstgb.ap().rearrange("p a b -> p (a b)")[:, 0:1024]), 16)

                    ccin_dma = sd.n

                    @blk.gpsimd
                    def _(g, ccin_dma=ccin_dma):
                        g.wait_ge(sd.h, ccin_dma)
                        scc.inc(g.collective_compute(
                            "ReduceScatter", ALU.add,
                            replica_groups=[[0, 1], [2, 3], [4, 5], [6, 7]],
                            ins=[d_ccin.ap().opt()], outs=[d_ccout.ap().opt()]))

        DO_D = "D" in phases
        w2q = stk.enter_context(nc.sbuf_tensor("w2q", [128, 32, DM], FP8, side="left"))

        with ExitStack() as st_d:
            scrd = st_d.enter_context(nc.sbuf_tensor("scrd", [128, 3, DM], F32, **R))
            sttd = st_d.enter_context(nc.sbuf_tensor("sttd", [128, 8], F32, **R))
            srd = st_d.enter_context(nc.sbuf_tensor("srd", [1, 132], F32, **R))
            p_d = st_d.enter_context(nc.psum_tensor("p_d", [128, 512], F32))

            with ExitStack() as st_d0:
                wstgd = st_d0.enter_context(nc.sbuf_tensor("wstgd", [128, 2, DM], F32, **R))
                wdscr = st_d0.enter_context(nc.sbuf_tensor("wdscr", [128, DM], F32, **R))

                with nc.Block() as blk:
                  if DO_D:
                    @blk.scalar
                    def _(s):
                        s.preload_act_table(AF.Sqrt)

                    @blk.sync
                    def _(q):
                        q.wait_ge(scc.h, 1)

                    @blk.vector
                    def _(v):
                        v.wait_ge(scc.h, 1)

                    @blk.gpsimd
                    def _(g):
                        g.wait_ge(scc.h, 1)

                    @blk.tensor
                    def _(t):
                        t.wait_ge(scc.h, 1)

                    w2_stage_v = sV.n
                    for hc in range(32):
                        _vf = w2_stage_v

                        @blk.sync
                        def _(q, hc=hc, _vf=_vf):
                            q.wait_ge(sV.h, _vf)
                            sd.inc(q.dma_start(
                                out=wstgd[:, hc % 2, :],
                                in_=d_w2t[128 * hc:128 * (hc + 1), :]), 16)
                        dmac = sd.n

                        @blk.vector
                        def _(v, hc=hc, dmac=dmac):
                            v.wait_ge(sd.h, dmac)
                            sV.inc(v.tensor_reduce(sttd[:, 0:1], wstgd[:, hc % 2, :],
                                                   AX.X, ALU.add,
                                                   apply_absolute_value=True))
                            v.wait_ge(sV.h, sV.n)
                            cp = 3 + 4 * (hc % 2)
                            cq = 3 + 4 * ((hc + 1) % 2)
                            if hc == 0:
                                sV.inc(v.tensor_copy(sttd[:, cp:cp + 1], sttd[:, 0:1]))
                            else:
                                sV.inc(v.tensor_add(sttd[:, cp:cp + 1],
                                                    sttd[:, cq:cq + 1], sttd[:, 0:1]))
                            v.wait_ge(sV.h, sV.n)
                        if hc % 2 == 1:
                            w2_stage_v = sV.n

                    prev = sV.n

                    @blk.sync
                    def _(q, prev=prev):
                        q.wait_ge(sV.h, prev)
                        sd.inc(q.dma_start(out=d_w2s.ap(), in_=sttd[:, 7:8]), 16)
                        q.wait_ge(sd.h, sd.n)
                        sd.inc(q.dma_start(out=srd[0:1, 0:128], in_=d_w2s.ap()), 16)
                    rowd = sd.n

                    @blk.vector
                    def _(v, rowd=rowd):
                        v.wait_ge(sd.h, rowd)
                        def VF(instr):
                            sV.inc(instr)
                            v.wait_ge(sV.h, sV.n)
                        VF(v.reduce_sum(srd[0:1, 128:129], srd[0:1, 0:128], AX.X))
                        VF(v.tensor_scalar(srd[0:1, 129:130], srd[0:1, 128:129],
                                           1.0 / (DM * HID), 1e-5, ALU.mult, ALU.max))
                        VF(v.reciprocal(srd[0:1, 130:131], srd[0:1, 129:130]))
                    rcd = sV.n

                    @blk.sync
                    def _(q, rcd=rcd):
                        q.wait_ge(sV.h, rcd)
                        sd.inc(q.dma_start(out=d_w2v.ap(), in_=srd[0:1, 129:131]), 16)
                        q.wait_ge(sd.h, sd.n)
                        sd.inc(q.dma_start(
                            out=s2bc[:, :],
                            in_=bass.AP(d_w2v.ap().tensor, 1, [[0, 128], [1, 1]])), 16)
                        sd.inc(q.dma_start(
                            out=scal[:, 1:2],
                            in_=bass.AP(d_w2v.ap().tensor, 0, [[0, 128], [1, 1]])), 16)
                    bcdma = sd.n

                    @blk.vector
                    def _(v, bcdma=bcdma):
                        v.wait_ge(sd.h, bcdma)
                        sV.inc(v.tensor_scalar_mul(r127sw2[:, :], scal[:, 1:2],
                                                   1.0 / 127.0))
                        v.wait_ge(sV.h, sV.n)
                    s2_done = sV.n

                    w2_stage_g = sV.n
                    for hc in range(32):
                        _gf = w2_stage_g

                        @blk.sync
                        def _(q, hc=hc, _gf=_gf, s2_done=s2_done):
                            q.wait_ge(sV.h, _gf)
                            q.wait_ge(sV.h, s2_done)
                            sd.inc(q.dma_start(
                                out=wstgd[:, hc % 2, :],
                                in_=d_w2t[128 * hc:128 * (hc + 1), :]), 16)
                        dmac = sd.n

                        @blk.vector
                        def _(v, hc=hc, dmac=dmac):
                            v.wait_ge(sd.h, dmac)
                            v.tensor_scalar(wdscr[:, :], wstgd[:, hc % 2, :],
                                            s2bc[:, 0:1], MAGIC,
                                            ALU.mult, ALU.add)
                            v.tensor_scalar(wstgd[:, hc % 2, :], wdscr[:, :],
                                            MAGIC, -1.0, ALU.subtract, ALU.max)
                            sV.inc(v.tensor_scalar_min(w2q[:, hc, :],
                                                       wstgd[:, hc % 2, :], 1.0))
                        if hc % 2 == 1:
                            w2_stage_g = sV.n

                    stg_free = sV.n
                    for jt in range(N_STAT):
                        r0 = 128 * jt
                        cnt = 128 if jt < 16 else DR - 2048
                        _df = stg_free

                        @blk.sync
                        def _(q, r0=r0, cnt=cnt, _df=_df):
                            q.wait_ge(scc.h, 1)
                            q.wait_ge(sV.h, _df)
                            sd.inc(q.dma_start(out=scrd[0:cnt, 0, :],
                                               in_=d_x0s[r0:r0 + cnt, :]), 16)
                            sd.inc(q.dma_start(out=scrd[0:cnt, 1, :],
                                               in_=d_ccout[r0:r0 + cnt, :]), 16)
                        dmac = sd.n

                        @blk.vector
                        def _(v, cnt=cnt, dmac=dmac):
                            v.wait_ge(sd.h, dmac)
                            sV.inc(v.tensor_add(scrd[0:cnt, 2, :], scrd[0:cnt, 0, :],
                                                scrd[0:cnt, 1, :]))
                        x1_done = sV.n

                        @blk.scalar
                        def _(s, cnt=cnt, x1_done=x1_done):
                            s.wait_ge(sV.h, x1_done)
                            sA.inc(s.activation(scrd[0:cnt, 1, :], scrd[0:cnt, 2, :],
                                                AF.Square, accum_out=sttd[0:cnt, 1:2]))
                        sq_done = sA.n

                        @blk.vector
                        def _(v, jt=jt, cnt=cnt, sq_done=sq_done):
                            def VF(instr):
                                sV.inc(instr)
                                v.wait_ge(sV.h, sV.n)
                            VF(v.reduce_sum(sttd[0:cnt, 0:1], scrd[0:cnt, 2, :], AX.X))
                            v.wait_ge(sA.h, sq_done)
                            VF(v.tensor_scalar_mul(m_arr[0:cnt, jt:jt + 1],
                                                   sttd[0:cnt, 0:1], 1.0 / DM))
                            VF(v.tensor_scalar_mul(sttd[0:cnt, 2:3], sttd[0:cnt, 1:2],
                                                   1.0 / DM))
                            VF(v.tensor_mul(sttd[0:cnt, 3:4], m_arr[0:cnt, jt:jt + 1],
                                            m_arr[0:cnt, jt:jt + 1]))
                            VF(v.tensor_sub(var_arr[0:cnt, jt:jt + 1], sttd[0:cnt, 2:3],
                                            sttd[0:cnt, 3:4]))
                            VF(v.tensor_single_scalar(scrd[0:cnt, 0, :], scrd[0:cnt, 2, :],
                                                      m_arr[0:cnt, jt:jt + 1],
                                                      ALU.subtract))
                            sV.inc(v.reduce_max(am_arr[0:cnt, jt:jt + 1],
                                                scrd[0:cnt, 0, :],
                                                AX.X, apply_absolute_value=True))
                        stg_free = sV.n
                    stats_done = sV.n

                    @blk.vector
                    def _(v, stats_done=stats_done):
                        v.wait_ge(sV.h, stats_done)
                        sV.inc(v.tensor_single_scalar(ds17a[:, :], var_arr[:, :],
                                                      epsap[:, 0:1], ALU.add))
                    veps_d = sV.n

                    @blk.scalar
                    def _(s, veps_d=veps_d):
                        s.wait_ge(sV.h, veps_d)
                        sA.inc(s.activation(ds17b[:, :], ds17a[:, :], AF.Sqrt,
                                            bias=zeroap[:, 0:1]))
                    sdn = sA.n

                    @blk.vector
                    def _(v, sdn=sdn):
                        v.wait_ge(sA.h, sdn)
                        def VF(instr):
                            sV.inc(instr)
                            v.wait_ge(sV.h, sV.n)
                        VF(v.reciprocal(ds17c[:, :], ds17b[:, :]))
                        VF(v.tensor_mul(var_arr[:, :], ds17a[:, :], ds17c[:, :]))
                        VF(v.tensor_add(ds17c[:, :], var_arr[:, :], ds17b[:, :]))
                        VF(v.tensor_scalar_mul(ds17b[:, :], ds17c[:, :], 0.5))
                        VF(v.reciprocal(ds17c[:, :], ds17b[:, :]))
                        VF(v.tensor_mul(ds17a[:, :], am_arr[:, :], ds17c[:, :]))
                        VF(v.tensor_scalar_max(am_arr[:, :], ds17a[:, :], 1e-5))
                        VF(v.tensor_scalar_mul(isx_arr[:, :], am_arr[:, :], 1.0 / 127.0))
                        VF(v.reciprocal(ds17b[:, :], isx_arr[:, :]))
                        sV.inc(v.tensor_mul(rsx_arr[:, :], ds17b[:, :], ds17c[:, :]))

            if "ccout" in taps:
                dst = tapt("ccout", (DR, DM))
                with nc.Block() as blk:
                    @blk.sync
                    def _(q):
                        q.wait_ge(scc.h, 1)
                        sd.inc(q.dma_start(out=dst, in_=d_ccout.ap()), 16)
                        q.wait_ge(sd.h, sd.n)

            with ExitStack() as st_d1:
                ynq_tm = st_d1.enter_context(nc.sbuf_tensor("ynq_tm", [128, 2, DM], BF16, **R))
                ynqT = st_d1.enter_context(nc.sbuf_tensor("ynqT", [128, 8, 258], BF16, **R))
                hb0 = st_d1.enter_context(nc.sbuf_tensor("hb0", [128, 32, 258], BF16, **R))
                hb1 = st_d1.enter_context(nc.sbuf_tensor("hb1", [128, 32, 258], BF16, **R))
                u_sb = st_d1.enter_context(nc.sbuf_tensor("u_sb", [128, 32, 128], F32, **R))
                u_sb2 = st_d1.enter_context(nc.sbuf_tensor("u_sb2", [128, 32, 128], F32, **R))
                uq = st_d1.enter_context(nc.sbuf_tensor("uq", [128, 32, 128], BF16, **R))
                ct1 = st_d1.enter_context(nc.sbuf_tensor("ct1", [128, 8, 128], F32, **R))
                ct2 = st_d1.enter_context(nc.sbuf_tensor("ct2", [128, 8, 128], F32, **R))
                ct3 = st_d1.enter_context(nc.sbuf_tensor("ct3", [128, 8, 128], F32, **R))
                mm1s = st_d1.enter_context(nc.sbuf_tensor("mm1s", [128, 128], F32, **R))
                amu = st_d1.enter_context(nc.sbuf_tensor("amu", [128, 2], F32, **R))
                fsc = st_d1.enter_context(nc.sbuf_tensor("fsc", [128, 16], F32, **R))
                beta = st_d1.enter_context(nc.sbuf_tensor("beta", [128, 2], F32, **R))
                shv = st_d1.enter_context(nc.sbuf_tensor("shv", [128, 2], F32, **R))
                shrow = st_d1.enter_context(nc.sbuf_tensor("shrow", [1, 128], F32, **R))
                shbc = st_d1.enter_context(nc.sbuf_tensor("shbc", [128, 128], F32, **R))
                xres = st_d1.enter_context(nc.sbuf_tensor("xres", [128, 2, DM], F32, **R))
                ares = st_d1.enter_context(nc.sbuf_tensor("ares", [128, 2, DM], F32, **R))
                outsb = st_d1.enter_context(nc.sbuf_tensor("outsb", [128, 2, DM], F32, **R))
                p_t2 = st_d1.enter_context(nc.psum_tensor("p_t2", [128, 2, 1024], BF16))
                p_m1 = st_d1.enter_context(nc.psum_tensor("p_m1", [128, 2, 512], F32))
                p_y = st_d1.enter_context(nc.psum_tensor("p_y", [128, 1024], F32))
                p_sh = st_d1.enter_context(nc.psum_tensor("p_sh", [128, 512], F32))

                hbs = [hb0, hb1]

                with nc.Block() as blk:
                  if DO_D:
                    @blk.scalar
                    def _(s):
                        s.preload_act_table(AF.Silu)

                with nc.Block() as blk:
                  if DO_D:
                    state = {
                        "stg_free": 0,
                        "ynqT_free": 0,
                        "ynqT_done": 0,
                        "pt2_copy": {},
                        "hb_conv_done": {},
                        "hb_evac": {},
                        "halo_done": {},
                        "u_free0": 0,
                        "u_free1": 0,
                        "uq_free": 0,
                        "py_free": 0,
                        "out_dma": 0,
                        "res_free": 0,
                        "round_done": 0,
                    }
                    w2q_ready = sV.n

                    class ConvEmitter:
                        def __init__(self, ob, sb, hd_v, hd_a, ubuf, chain=None):
                            self.hb = hbs[ob % 2]
                            self.c0 = 128 * sb
                            self.hd_v, self.hd_a = hd_v, hd_a
                            self.ubuf = ubuf
                            self.v_done = chain.v_done if chain else 0
                            self.g_done = {}
                            self.silu = {}
                            if chain:
                                self.silu[-1] = chain.silu[3]
                            self.rd = state["round_done"]
                            self.uf = state["u_free0" if sb == 0 else "u_free1"]
                            self.silu_done = 0

                        def emit_batch(self, bi, extra_wait=None):
                            hcs = range(8 * bi, 8 * bi + 8)
                            hb, c0 = self.hb, self.c0
                            _pv = self.v_done
                            _hv, _ha, _rd = self.hd_v, self.hd_a, self.rd

                            @blk.scalar
                            def _(s, hcs=hcs, _pv=_pv, bi=bi, c0=c0, hb=hb,
                                  _hv=_hv, _ha=_ha, _rd=_rd, extra_wait=extra_wait):
                                if bi == 0:
                                    s.wait_ge(sV.h, _hv)
                                    if _ha:
                                        s.wait_ge(sA.h, _ha)
                                    if _rd:
                                        s.wait_ge(sV.h, _rd)
                                if extra_wait is not None:
                                    s.wait_ge(sV.h, extra_wait[0])
                                    s.wait_ge(sA.h, extra_wait[1])
                                if _pv:
                                    s.wait_ge(sV.h, _pv)
                                for hc in hcs:
                                    sA.inc(s.activation(
                                        ct1[:, hc % 8, :],
                                        hb[:, hc, c0 + 2:c0 + 130],
                                        AF.Identity,
                                        bias=cbp[:, hc:hc + 1],
                                        scale=cwq[:, 3 * hc + 2:3 * hc + 3]))
                            a_done = sA.n
                            self.emit_silu(bi - 1)

                            @blk.vector
                            def _(v, hcs=hcs, a_done=a_done, c0=c0, hb=hb):
                                v.wait_ge(sA.h, a_done)
                                for hc in hcs:
                                    sV.inc(v.scalar_tensor_tensor(
                                        ct2[:, hc % 8, :],
                                        hb[:, hc, c0 + 1:c0 + 129],
                                        cwq[:, 3 * hc + 1:3 * hc + 2],
                                        ct1[:, hc % 8, :], ALU.mult, ALU.add))
                            self.v_done = sV.n
                            _ps = self.silu.get(bi - 1) or 0

                            @blk.vector
                            def _(v, hcs=hcs, _ps=_ps, c0=c0, hb=hb):
                                if _ps:
                                    v.wait_ge(sA.h, _ps)
                                for hc in hcs:
                                    sV.inc(v.scalar_tensor_tensor(
                                        ct3[:, hc % 8, :],
                                        hb[:, hc, c0:c0 + 128],
                                        cwq[:, 3 * hc:3 * hc + 1],
                                        ct2[:, hc % 8, :], ALU.mult, ALU.add))
                            self.g_done[bi] = sV.n

                        def emit_silu(self, bi):
                            if bi < 0 or bi in self.silu:
                                return
                            g_done = self.g_done[bi]
                            _uf = self.uf
                            hcs = range(8 * bi, 8 * bi + 8)
                            ubuf = self.ubuf

                            @blk.scalar
                            def _(s, hcs=hcs, g_done=g_done, _uf=_uf, bi=bi,
                                  ubuf=ubuf):
                                s.wait_ge(sV.h, g_done)
                                if bi == 0 and _uf:
                                    s.wait_ge(sV.h, _uf)
                                for hc in hcs:
                                    sA.inc(s.activation(ubuf[:, hc, :],
                                                        ct3[:, hc % 8, :], AF.Silu))
                            self.silu[bi] = sA.n

                        def finish(self):
                            self.emit_silu(3)
                            self.silu_done = self.silu[3]

                    def emit_stage(i):
                        sz = HB_SZ[i]
                        tiles = [(2 * i, 0, 128), (2 * i + 1, 1, 128)]
                        if sz == 258:
                            tiles.append((2 * i + 2, 2, 2))
                        _ytf = state["ynqT_free"]
                        ynq_done_tile = []
                        for ti, (jt, sl, cnt) in enumerate(tiles):
                            _sf = state["stg_free"]

                            @blk.sync
                            def _(q, jt=jt, cnt=cnt, _sf=_sf):
                                q.wait_ge(scc.h, 1)
                                q.wait_ge(sV.h, _sf)
                                sd.inc(q.dma_start(
                                    out=scrd[0:cnt, 0, :],
                                    in_=d_x0s[128 * jt:128 * jt + cnt, :]), 16)
                                sd.inc(q.dma_start(
                                    out=scrd[0:cnt, 1, :],
                                    in_=d_ccout[128 * jt:128 * jt + cnt, :]), 16)
                            dmac = sd.n

                            @blk.vector
                            def _(v, jt=jt, ti=ti, cnt=cnt, dmac=dmac):
                                v.wait_ge(sd.h, dmac)
                                v.tensor_add(scrd[0:cnt, 2, :], scrd[0:cnt, 0, :],
                                             scrd[0:cnt, 1, :])
                                v.tensor_scalar(scrd[0:cnt, 0, :], scrd[0:cnt, 2, :],
                                                m_arr[0:cnt, jt:jt + 1],
                                                rsx_arr[0:cnt, jt:jt + 1],
                                                ALU.subtract, ALU.mult)
                                v.tensor_scalar(scrd[0:cnt, 1, :], scrd[0:cnt, 0, :],
                                                MAGIC, MAGIC, ALU.add, ALU.subtract)
                                sV.inc(v.tensor_scalar_mul(ynq_tm[0:cnt, ti % 2, :],
                                                           scrd[0:cnt, 1, :],
                                                           isx_arr[0:cnt, jt:jt + 1]))
                            ynq_done_tile.append(sV.n)
                            state["stg_free"] = sV.n

                            _cp = state["pt2_copy"].get(ti % 2, 0)
                            _yd = ynq_done_tile[ti]

                            @blk.tensor
                            def _(t, ti=ti, cnt=cnt, _yd=_yd, _cp=_cp, _ytf=_ytf):
                                t.wait_ge(sV.h, _yd)
                                if _cp:
                                    t.wait_ge(sV.h, _cp)
                                if ti == 0 and _ytf:
                                    t.wait_ge(sT.h, _ytf)
                                mm = None
                                for dc in range(8):
                                    mm = t.transpose(
                                        p_t2[:, ti % 2, 128 * dc:128 * dc + cnt],
                                        ynq_tm[0:cnt, ti % 2, dc * 128:(dc + 1) * 128],
                                        eyeb[0:cnt, 0:cnt])
                                sT.inc(mm)
                            trb = sT.n

                            @blk.vector
                            def _(v, ti=ti, cnt=cnt, trb=trb):
                                v.wait_ge(sT.h, trb)
                                sV.inc(v.tensor_copy(
                                    ynqT[:, 0:8, 128 * ti:128 * ti + cnt],
                                    p_t2[:, ti % 2, :].rearrange(
                                        "p (a b) -> p a b", a=8)[:, :, 0:cnt]))
                            state["pt2_copy"][ti % 2] = sV.n
                        state["ynqT_done"] = sV.n

                    def emit_m1(i, ob):
                        sz = HB_SZ[i]
                        ce0 = ce1 = None
                        if ob is not None:
                            hv, ha = state["hb_evac"][ob]
                            ce0 = ConvEmitter(ob, 0, hv, ha, u_sb)
                        ynqT_done = state["ynqT_done"]
                        evac_at = [None] * 32
                        evhalo = [None] * 32
                        weave_pos = {3: 0, 7: 1, 11: 2, 15: 3,
                                     19: 4, 23: 5, 27: 6, 31: 7}
                        for hc in range(32):
                            _w = evac_at[hc - 2] if hc >= 2 else None
                            _cvd = state["hb_conv_done"].get(i - 2)

                            @blk.tensor
                            def _(t, hc=hc, sz=sz, ynqT_done=ynqT_done, _w=_w):
                                if hc == 0:
                                    t.wait_ge(sV.h, ynqT_done)
                                    t.wait_ge(sV.h, w2q_ready)
                                if _w is not None:
                                    t.wait_ge(sV.h if _w[0] == "V" else sA.h, _w[1])
                                mm = None
                                for dc in range(8):
                                    mm = t.matmul(p_m1[:, hc % 2, 0:sz],
                                                  w1q[:, dc, 128 * hc:128 * hc + 128],
                                                  ynqT[:, dc, 0:sz],
                                                  start=(dc == 0), stop=(dc == 7))
                                sT.inc(mm)
                            m1_done = sT.n

                            if hc % 2 == 0:
                                @blk.vector
                                def _(v, hc=hc, sz=sz, i=i, m1_done=m1_done, _cvd=_cvd):
                                    v.wait_ge(sT.h, m1_done)
                                    if hc == 0 and _cvd is not None:
                                        v.wait_ge(sA.h, _cvd[0])
                                    sV.inc(v.tensor_copy(hbs[i % 2][:, hc, 0:sz],
                                                         p_m1[:, hc % 2, 0:sz]))
                                    if i >= 1:
                                        sV.inc(v.tensor_copy(
                                            hbs[(i - 1) % 2][:, hc, 256:258],
                                            p_m1[:, hc % 2, 0:2]))
                                evac_at[hc] = ("V", sV.n)
                            else:
                                @blk.scalar
                                def _(s, hc=hc, sz=sz, i=i, m1_done=m1_done, _cvd=_cvd):
                                    s.wait_ge(sT.h, m1_done)
                                    if hc == 1 and _cvd is not None:
                                        s.wait_ge(sA.h, _cvd[0])
                                    sA.inc(s.activation(hbs[i % 2][:, hc, 0:sz],
                                                        p_m1[:, hc % 2, 0:sz], AF.Copy))
                                    if i >= 1:
                                        sA.inc(s.activation(
                                            hbs[(i - 1) % 2][:, hc, 256:258],
                                            p_m1[:, hc % 2, 0:2], AF.Copy))
                                evac_at[hc] = ("A", sA.n)

                            evhalo[hc] = (sV.n, sA.n)

                            if ce0 is not None and hc in weave_pos:
                                bi = weave_pos[hc]
                                if bi < 4:
                                    ce0.emit_batch(bi)
                                else:
                                    if bi == 4:
                                        ce0.finish()
                                        ce1 = ConvEmitter(ob, 1, 0, 0, u_sb2,
                                                          chain=ce0)
                                    ce1.emit_batch(bi - 4,
                                                   extra_wait=evhalo[8 * (bi - 4) + 7])
                        if ce1 is not None:
                            ce1.finish()
                        state["hb_evac"][i] = (sV.n, sA.n)
                        if i >= 1:
                            state["halo_done"][i - 1] = (sV.n, sA.n)
                        state["ynqT_free"] = sT.n
                        return (ce0.silu_done, ce1.silu_done) if ce0 is not None else None

                    def emit_sb_tail(ob, sb, silu_done, res_dma, ubuf):
                        @blk.vector
                        def _(v, silu_done=silu_done, ubuf=ubuf):
                            v.wait_ge(sA.h, silu_done)
                            sV.inc(v.tensor_reduce(
                                mm1s[:, :],
                                ubuf.ap().rearrange("p c j -> p j c"),
                                AX.X, ALU.max, apply_absolute_value=True))
                        red_done = sV.n

                        @blk.tensor
                        def _(t, red_done=red_done):
                            t.wait_ge(sV.h, red_done)
                            sT.inc(t.transpose(p_sh[:, 0:128], mm1s[:, :],
                                               eyef[:, :]))
                        tr_done = sT.n

                        @blk.vector
                        def _(v, tr_done=tr_done, sb=sb):
                            v.wait_ge(sT.h, tr_done)
                            def VF(instr):
                                sV.inc(instr)
                                v.wait_ge(sV.h, sV.n)
                            VF(v.reduce_max(amu[:, sb:sb + 1], p_sh[:, 0:128], AX.X))
                            VF(v.tensor_scalar_max(shv[:, sb:sb + 1],
                                                   amu[:, sb:sb + 1], 1e-5))
                            VF(v.tensor_single_scalar(beta[:, sb:sb + 1],
                                                      shv[:, sb:sb + 1],
                                                      r127sw2[:, 0:1], ALU.mult))
                            VF(v.reciprocal(amu[:, sb:sb + 1], shv[:, sb:sb + 1]))
                            VF(v.tensor_scalar_mul(shv[:, sb:sb + 1],
                                                   amu[:, sb:sb + 1], 127.0))
                            VF(v.tensor_copy(fsc[:, 0:1], shv[:, sb:sb + 1]))
                            VF(v.tensor_copy(fsc[:, 1:2], fsc[:, 0:1]))
                        sh_done = sV.n

                        @blk.tensor
                        def _(t, sh_done=sh_done, sb=sb):
                            t.wait_ge(sV.h, sh_done)
                            sT.inc(t.transpose(p_sh[0:1, 128:256],
                                               shv[:, sb:sb + 1], eyef[:, :]))
                        shr_done = sT.n

                        @blk.vector
                        def _(v, shr_done=shr_done):
                            v.wait_ge(sT.h, shr_done)
                            sV.inc(v.tensor_copy(shrow[0:1, :], p_sh[0:1, 128:256]))
                            v.wait_ge(sV.h, sV.n)
                            sV.inc(v.tensor_copy(fsc[0:1, 0:8], shrow[0:1, 0:8]))
                            v.wait_ge(sV.h, sV.n)
                            sV.inc(v.tensor_copy(fsc[0:1, 8:16], shrow[0:1, 8:16]))
                        shc_done = sV.n

                        @blk.tensor
                        def _(t, shc_done=shc_done):
                            t.wait_ge(sV.h, shc_done)
                            sT.inc(t.matmul(p_sh[:, 256:384], onesf[0:1, :],
                                            shrow[0:1, :], start=True, stop=True))
                        shb_done = sT.n

                        @blk.vector
                        def _(v, shb_done=shb_done):
                            v.wait_ge(sT.h, shb_done)
                            sV.inc(v.tensor_copy(shbc[:, :], p_sh[:, 256:384]))
                        shbc_done = sV.n

                        _uqf = state["uq_free"]
                        for bi in range(4):
                            hcs = range(8 * bi, 8 * bi + 8)

                            @blk.vector
                            def _(v, hcs=hcs, shbc_done=shbc_done, _uqf=_uqf, bi=bi,
                                  ubuf=ubuf):
                                if bi == 0:
                                    v.wait_ge(sV.h, shbc_done)
                                    if _uqf:
                                        v.wait_ge(sT.h, _uqf)
                                for hc in hcs:
                                    v.tensor_mul(ct1[:, hc % 8, :],
                                                 ubuf[:, hc, :], shbc[:, :])
                                for hc in hcs:
                                    sV.inc(v.tensor_scalar(uq[:, hc, :],
                                                           ct1[:, hc % 8, :],
                                                           MAGIC, MAGIC,
                                                           ALU.add, ALU.subtract))
                        uq_done = sV.n
                        state["u_free0" if sb == 0 else "u_free1"] = sV.n
                        state["round_done"] = sV.n

                        _pyf = state["py_free"]

                        @blk.tensor
                        def _(t, uq_done=uq_done, _pyf=_pyf):
                            t.wait_ge(sV.h, uq_done)
                            if _pyf:
                                t.wait_ge(sV.h, _pyf)
                            for n in range(2):
                                mm = None
                                for hc in range(32):
                                    mm = t.matmul(
                                        p_y[:, 512 * n:512 * n + 512],
                                        uq[:, hc, :],
                                        w2q[:, hc, 512 * n:512 * n + 512],
                                        start=(hc == 0), stop=(hc == 31))
                                sT.inc(mm)
                        y_done = sT.n
                        state["uq_free"] = sT.n

                        _od = state["out_dma"]

                        @blk.vector
                        def _(v, sb=sb, y_done=y_done, res_dma=res_dma, _od=_od):
                            v.wait_ge(sT.h, y_done)
                            v.wait_ge(sd.h, res_dma)
                            if _od:
                                v.wait_ge(sd.h, _od)
                            ct2f = ct2.ap().rearrange("p a b -> p (a b)")
                            v.scalar_tensor_tensor(
                                ct2f, p_y[:, :],
                                beta[:, sb:sb + 1], xres[:, sb, :],
                                ALU.mult, ALU.add)
                            sV.inc(v.tensor_add(outsb[:, sb, :], ct2f,
                                                ares[:, sb, :]))
                        state["py_free"] = sV.n

                    def emit_out_block(ob, woven_silu=None):
                        hd = state["halo_done"].get(ob)
                        if hd is None:
                            hd = state["hb_evac"][ob]

                        _rf = state["res_free"]

                        @blk.sync
                        def _(q, ob=ob, _rf=_rf):
                            q.wait_ge(sV.h, _rf)
                            r0 = 256 * ob + 1
                            sd.inc(q.dma_start(
                                out=xres[:, :, :],
                                in_=d_x0s[r0:r0 + 256, :].rearrange(
                                    "(tc p) d -> p tc d", p=128)), 16)
                            sd.inc(q.dma_start(
                                out=ares[:, :, :],
                                in_=d_ccout[r0:r0 + 256, :].rearrange(
                                    "(tc p) d -> p tc d", p=128)), 16)
                        res_dma = sd.n

                        if woven_silu is None:
                            hv, ha = state["hb_evac"][ob]
                            ce0 = ConvEmitter(ob, 0, hv, ha, u_sb)
                            for bi in range(4):
                                ce0.emit_batch(bi)
                            ce0.finish()
                            ce1 = ConvEmitter(ob, 1, hd[0], hd[1], u_sb2,
                                              chain=ce0)
                            for bi in range(4):
                                ce1.emit_batch(bi)
                            ce1.finish()
                            silu0, silu1 = ce0.silu_done, ce1.silu_done
                        else:
                            silu0, silu1 = woven_silu
                        emit_sb_tail(ob, 0, silu0, res_dma, u_sb)
                        emit_sb_tail(ob, 1, silu1, res_dma, u_sb2)

                        ev_done = sV.n
                        state["res_free"] = sV.n

                        @blk.sync
                        def _(q, ob=ob, ev_done=ev_done):
                            q.wait_ge(sV.h, ev_done)
                            sd.inc(q.dma_start(
                                out=d_out[256 * ob:256 * ob + 256, :].rearrange(
                                    "(tc p) d -> p tc d", p=128),
                                in_=outsb[:, :, :]), 16)
                        state["out_dma"] = sd.n

                        state["hb_conv_done"][ob] = (sA.n, sV.n, sG.n)

                    emit_stage(0)
                    emit_m1(0, None)
                    for i in range(1, 8):
                        emit_stage(i)
                        ws = emit_m1(i, i - 1)
                        emit_out_block(i - 1, woven_silu=ws)
                    emit_out_block(7)

                    t_w1 = tapt("w1q", (128, 8 * HID))
                    t_w2 = tapt("w2q", (128, 32 * DM))
                    t_st = tapt("dstats", (128, 96))
                    if t_w1 is not None:
                        for part in range(8):
                            @blk.vector
                            def _(v, part=part):
                                v.wait_ge(sd.h, state["out_dma"])
                                sV.inc(v.tensor_copy(
                                    xres.ap().rearrange("p a b -> p (a b)")[:, 0:2048]
                                    if False else xres.ap().rearrange("p a b -> p (a b)")[:, 0:2048],
                                    w1q.ap().rearrange("p a b -> p (a b)")[
                                        :, 4096 * part:4096 * part + 2048]))
                                sV.inc(v.tensor_copy(
                                    ares.ap().rearrange("p a b -> p (a b)")[:, 0:2048],
                                    w1q.ap().rearrange("p a b -> p (a b)")[
                                        :, 4096 * part + 2048:4096 * part + 4096]))
                            vtap = sV.n

                            @blk.sync
                            def _(q, part=part, vtap=vtap):
                                q.wait_ge(sV.h, vtap)
                                sd.inc(q.dma_start(
                                    out=t_w1[:, 4096 * part:4096 * part + 2048],
                                    in_=xres.ap().rearrange("p a b -> p (a b)")[:, 0:2048]), 16)
                                sd.inc(q.dma_start(
                                    out=t_w1[:, 4096 * part + 2048:4096 * part + 4096],
                                    in_=ares.ap().rearrange("p a b -> p (a b)")[:, 0:2048]), 16)

                            @blk.vector
                            def _(v):
                                v.wait_ge(sd.h, sd.n)
                    if t_w2 is not None:
                        for part in range(8):
                            @blk.vector
                            def _(v, part=part):
                                v.wait_ge(sd.h, sd.n)
                                sV.inc(v.tensor_copy(
                                    xres.ap().rearrange("p a b -> p (a b)")[:, 0:2048],
                                    w2q.ap().rearrange("p a b -> p (a b)")[
                                        :, 4096 * part:4096 * part + 2048]))
                                sV.inc(v.tensor_copy(
                                    ares.ap().rearrange("p a b -> p (a b)")[:, 0:2048],
                                    w2q.ap().rearrange("p a b -> p (a b)")[
                                        :, 4096 * part + 2048:4096 * part + 4096]))
                            vtap = sV.n

                            @blk.sync
                            def _(q, part=part, vtap=vtap):
                                q.wait_ge(sV.h, vtap)
                                sd.inc(q.dma_start(
                                    out=t_w2[:, 4096 * part:4096 * part + 2048],
                                    in_=xres.ap().rearrange("p a b -> p (a b)")[:, 0:2048]), 16)
                                sd.inc(q.dma_start(
                                    out=t_w2[:, 4096 * part + 2048:4096 * part + 4096],
                                    in_=ares.ap().rearrange("p a b -> p (a b)")[:, 0:2048]), 16)

                            @blk.vector
                            def _(v):
                                v.wait_ge(sd.h, sd.n)
                    if t_st is not None:
                        @blk.vector
                        def _(v):
                            v.wait_ge(sd.h, sd.n)
                            v.tensor_copy(outsb[:, 0, 0:17], m_arr[:, :])
                            v.tensor_copy(outsb[:, 0, 17:34], rsx_arr[:, :])
                            v.tensor_copy(outsb[:, 0, 34:51], isx_arr[:, :])
                            v.tensor_copy(outsb[:, 0, 51:68], am_arr[:, :])
                            v.tensor_copy(outsb[:, 0, 68:69], s1bc[:, :])
                            v.tensor_copy(outsb[:, 0, 69:70], s2bc[:, :])
                            v.tensor_copy(outsb[:, 0, 70:71], r127sw2[:, :])
                            v.tensor_copy(outsb[:, 0, 71:73], beta[:, :])
                            v.tensor_copy(outsb[:, 0, 73:75], shv[:, :])
                            v.tensor_copy(outsb[:, 0, 75:78], scal[:, 0:3])
                            sV.inc(v.tensor_copy(outsb[:, 0, 80:88], sttd[:, 0:8]))
                        vtap = sV.n

                        @blk.sync
                        def _(q, vtap=vtap):
                            q.wait_ge(sV.h, vtap)
                            sd.inc(q.dma_start(out=t_st[:, 0:96],
                                               in_=outsb[:, 0, 0:96]), 16)

                    @blk.sync
                    def _(q):
                        q.wait_ge(sd.h, sd.n)

    return nc, tapd



def kernel(L0_attn_x, L0_attn_w_qkv, L0_attn_w_out, L0_ffn_w1, L0_ffn_conv_w,
           L0_ffn_conv_b, L0_ffn_w2):
    x = np.asarray(L0_attn_x, dtype=np.float32)
    in_maps = _stage_inputs(x,
                            np.asarray(L0_attn_w_qkv, dtype=np.float32),
                            np.asarray(L0_attn_w_out, dtype=np.float32),
                            np.asarray(L0_ffn_w1, dtype=np.float32),
                            np.asarray(L0_ffn_conv_w, dtype=np.float32),
                            np.asarray(L0_ffn_conv_b, dtype=np.float32),
                            np.asarray(L0_ffn_w2, dtype=np.float32))
    nc, _ = build_program()
    res = run_bass_kernel_spmd(nc, in_maps, core_ids=list(range(NCORE)))
    out = np.empty((4, S, DM), dtype=np.float32)
    for core in range(NCORE):
        b, hh = core // 2, core % 2
        out[b, hh * OWN:(hh + 1) * OWN] = res.results[core]["out"]
    return out


# version 09
# speedup vs baseline: 1.1386x; 1.1386x over previous
import numpy as np
from contextlib import ExitStack

import concourse.bass as bass
import concourse.mybir as mybir
from concourse.bass_utils import run_bass_kernel_spmd

F32 = mybir.dt.float32
BF16 = mybir.dt.bfloat16
FP8 = mybir.dt.float8e4
AF = mybir.ActivationFunctionType
ALU = mybir.AluOpType
AX = mybir.AxisListType

MAGIC = 12582912.0
S, DM, HID, DH = 4096, 1024, 4096, 64
NCORE, OWN = 8, 2048
DR = OWN + 2
EPS = 1e-5
A_CH, N_ACH = 512, 8
QB, N_QB = 256, 16
D_BLK, N_DBLK = 256, 8
HB_SZ = [256] * 7 + [258]
N_STAT = 17



def _rope_tables():
    inv_freq = (1.0 / (np.float32(10000.0) **
                       (np.arange(0, DH, 2, dtype=np.float32) / np.float32(DH))))
    t = np.arange(S, dtype=np.float32)
    freqs = t[:, None] * inv_freq[None, :]
    cos = np.cos(freqs.astype(np.float64)).astype(np.float32)
    sin = np.sin(freqs.astype(np.float64)).astype(np.float32)
    return (np.ascontiguousarray(np.tile(cos.T, (4, 1))),
            np.ascontiguousarray(np.tile(sin.T, (4, 1))))


def _stage_inputs(x, w_qkv, w_out, w1, conv_w, conv_b, w2):
    cos4, sin4 = _rope_tables()
    eye = np.eye(128, dtype=np.float32)
    w1t = np.ascontiguousarray(w1.T)
    w2t = np.ascontiguousarray(w2.T)
    cwm = np.ascontiguousarray(conv_w.reshape(32, 128, 3).transpose(1, 0, 2)
                               .reshape(128, 96))
    cbm = np.ascontiguousarray(conv_b.reshape(32, 128).T)
    in_maps = []
    for core in range(NCORE):
        b, hh = core // 2, core % 2
        heads = [hh * 8 + j for j in range(8)]
        qcols = []
        for g in range(2):
            hs = heads[4 * g:4 * g + 4]
            for h in hs:
                qcols.extend(range(h * DH, h * DH + 32))
            for h in hs:
                qcols.extend(range(h * DH + 32, h * DH + 64))
        kcols = [c + DM for c in qcols]
        vcols = []
        for h in heads:
            vcols.extend(range(2 * DM + h * DH, 2 * DM + (h + 1) * DH))
        own0 = hh * OWN
        x0s = np.zeros((DR, DM), dtype=np.float32)
        lo, hi = own0 - 1, own0 + OWN + 1
        clo, chi = max(lo, 0), min(hi, S)
        x0s[clo - lo:chi - lo] = x[b, clo:chi]
        in_maps.append({
            "xfull": np.ascontiguousarray(x[b]),
            "x0s": x0s,
            "wq": np.ascontiguousarray(w_qkv[:, qcols]),
            "wk": np.ascontiguousarray(w_qkv[:, kcols]),
            "wv": np.ascontiguousarray(w_qkv[:, vcols]),
            "wo": np.ascontiguousarray(w_out[hh * 512:(hh + 1) * 512, :]),
            "w1t": w1t, "w2t": w2t, "cwm": cwm, "cbm": cbm,
            "cost": cos4, "sint": sin4, "eyef": eye,
        })
    return in_maps



class Sem:
    def __init__(self, stk, nc, name):
        self.h = stk.enter_context(nc.semaphore(name))
        self.n = 0

    def inc(self, instr, k=1):
        instr.then_inc(self.h, k)
        self.n += k
        return instr


def build_program(taps=(), phases="ABD"):
    nc = bass.Bass("TRN2", target_bir_lowering=False, enable_partition_id=False)
    taps = set(taps)
    tapd = {}

    d_x = nc.dram_tensor("xfull", [S, DM], F32, kind="ExternalInput").ap()
    d_x0s = nc.dram_tensor("x0s", [DR, DM], F32, kind="ExternalInput").ap()
    d_wq = nc.dram_tensor("wq", [DM, 512], F32, kind="ExternalInput").ap()
    d_wk = nc.dram_tensor("wk", [DM, 512], F32, kind="ExternalInput").ap()
    d_wv = nc.dram_tensor("wv", [DM, 512], F32, kind="ExternalInput").ap()
    d_wo = nc.dram_tensor("wo", [512, DM], F32, kind="ExternalInput").ap()
    d_w1t = nc.dram_tensor("w1t", [DM, HID], F32, kind="ExternalInput").ap()
    d_w2t = nc.dram_tensor("w2t", [HID, DM], F32, kind="ExternalInput").ap()
    d_cwm = nc.dram_tensor("cwm", [128, 96], F32, kind="ExternalInput").ap()
    d_cbm = nc.dram_tensor("cbm", [128, 32], F32, kind="ExternalInput").ap()
    d_cos = nc.dram_tensor("cost", [128, S], F32, kind="ExternalInput").ap()
    d_sin = nc.dram_tensor("sint", [128, S], F32, kind="ExternalInput").ap()
    d_eye = nc.dram_tensor("eyef", [128, 128], F32, kind="ExternalInput").ap()
    d_out = nc.dram_tensor("out", [OWN, DM], F32, kind="ExternalOutput").ap()
    d_ccin = nc.dram_tensor("cc_in", [2 * DR, DM], F32)
    d_ccout = nc.dram_tensor("cc_out", [DR, DM], F32)
    d_w1s = nc.dram_tensor("w1s", [1, 128], F32)
    d_w1v = nc.dram_tensor("w1v", [1, 2], F32)
    d_w2s = nc.dram_tensor("w2s", [1, 128], F32)
    d_w2v = nc.dram_tensor("w2v", [1, 2], F32)

    def tapt(name, shape, dtype=F32):
        if name in taps:
            tapd[name] = nc.dram_tensor("tap_" + name, list(shape), dtype,
                                        kind="ExternalOutput").ap()
            return tapd[name]
        return None

    stk = ExitStack()
    with stk:
        L = dict(side="left")
        R = dict(side="right")
        eyeb = stk.enter_context(nc.sbuf_tensor("eyeb", [128, 128], BF16, **L))
        eyef = stk.enter_context(nc.sbuf_tensor("eyefs", [128, 128], F32, **L))
        onesf = stk.enter_context(nc.sbuf_tensor("onesf", [128, 128], F32, **L))
        onesb = stk.enter_context(nc.sbuf_tensor("onesb", [128, 1], BF16, **L))
        zerob = stk.enter_context(nc.sbuf_tensor("zerob", [1, 512], BF16, **L))
        zerof = stk.enter_context(nc.sbuf_tensor("zerof", [1, DM], F32, **L))
        epsap = stk.enter_context(nc.sbuf_tensor("epsap", [128, 1], F32, **L))
        zeroap = stk.enter_context(nc.sbuf_tensor("zeroap", [128, 1], F32, **L))
        cwp = stk.enter_context(nc.sbuf_tensor("cwp", [128, 96], F32, **L))
        cwq = stk.enter_context(nc.sbuf_tensor("cwq", [128, 96], F32, **L))
        cbp = stk.enter_context(nc.sbuf_tensor("cbp", [128, 32], F32, **L))
        m_arr = stk.enter_context(nc.sbuf_tensor("m_arr", [128, N_STAT], F32, **L))
        var_arr = stk.enter_context(nc.sbuf_tensor("var_arr", [128, N_STAT], F32, **L))
        am_arr = stk.enter_context(nc.sbuf_tensor("am_arr", [128, N_STAT], F32, **L))
        rsx_arr = stk.enter_context(nc.sbuf_tensor("rsx_arr", [128, N_STAT], F32, **L))
        isx_arr = stk.enter_context(nc.sbuf_tensor("isx_arr", [128, N_STAT], F32, **L))
        ds17a = stk.enter_context(nc.sbuf_tensor("ds17a", [128, N_STAT], F32, **L))
        ds17b = stk.enter_context(nc.sbuf_tensor("ds17b", [128, N_STAT], F32, **L))
        ds17c = stk.enter_context(nc.sbuf_tensor("ds17c", [128, N_STAT], F32, **L))
        s1bc = stk.enter_context(nc.sbuf_tensor("s1bc", [128, 1], F32, **L))
        s2bc = stk.enter_context(nc.sbuf_tensor("s2bc", [128, 1], F32, **L))
        r127sw2 = stk.enter_context(nc.sbuf_tensor("r127sw2", [128, 1], F32, **L))
        scal = stk.enter_context(nc.sbuf_tensor("scal", [128, 8], F32, **L))

        sd = Sem(stk, nc, "sd")
        sd2 = Sem(stk, nc, "sd2")
        sT = Sem(stk, nc, "sT")
        sV = Sem(stk, nc, "sV")
        sA = Sem(stk, nc, "sA")
        sG = Sem(stk, nc, "sG")
        sE = Sem(stk, nc, "sE")
        sM = Sem(stk, nc, "sM")
        sAV = Sem(stk, nc, "sAV")
        sZR = Sem(stk, nc, "sZR")
        sBC = Sem(stk, nc, "sBC")
        sOC = Sem(stk, nc, "sOC")
        sOM = Sem(stk, nc, "sOM")
        sWO = Sem(stk, nc, "sWO")
        sAO = Sem(stk, nc, "sAO")
        scc = Sem(stk, nc, "scc")

        with nc.Block() as blk:
            @blk.sync
            def _(q):
                sd.inc(q.dma_start(out=eyef[:, :], in_=d_eye), 16)
                sd.inc(q.dma_start(out=cwp[:, :], in_=d_cwm), 16)
                sd.inc(q.dma_start(out=cbp[:, :], in_=d_cbm), 16)

            @blk.gpsimd
            def _(g):
                g.memset(onesf[:, :], 1.0)
                g.memset(onesb[:, :], 1.0)
                g.memset(zerob[:, :], 0.0)
                g.memset(zerof[:, :], 0.0)
                g.memset(epsap[:, :], EPS)
                g.memset(zeroap[:, :], 0.0)

            @blk.vector
            def _(v):
                v.wait_ge(sd.h, sd.n)
                v.tensor_copy(eyeb[:, :], eyef[:, :])

        with ExitStack() as st_ab:
            qx1 = st_ab.enter_context(nc.sbuf_tensor("qx1", [128, 2, S], BF16, **R))
            qx2 = st_ab.enter_context(nc.sbuf_tensor("qx2", [128, 2, S], BF16, **R))
            kx1 = st_ab.enter_context(nc.sbuf_tensor("kx1", [128, 2, S], BF16, **R))
            kx2 = st_ab.enter_context(nc.sbuf_tensor("kx2", [128, 2, S], BF16, **R))
            v_tm = st_ab.enter_context(nc.sbuf_tensor("v_tm", [128, 32, 512], BF16, **R))
            wo_sb = st_ab.enter_context(nc.sbuf_tensor("wo_sb", [128, 4, DM], BF16, **R))

            with ExitStack() as st_a:
                wq_sb = st_a.enter_context(nc.sbuf_tensor("wq_sb", [128, 8, 512], BF16, **R))
                wk_sb = st_a.enter_context(nc.sbuf_tensor("wk_sb", [128, 8, 512], BF16, **R))
                wv_sb = st_a.enter_context(nc.sbuf_tensor("wv_sb", [128, 8, 512], BF16, **R))
                wstg = st_a.enter_context(nc.sbuf_tensor("wstg", [128, 8, 512], F32, **R))
                xa = st_a.enter_context(nc.sbuf_tensor("xa", [128, 4, DM], F32, **R))
                xn_tm = st_a.enter_context(nc.sbuf_tensor("xn_tm", [128, 4, DM], BF16, **R))
                xnf = st_a.enter_context(nc.sbuf_tensor("xnf", [128, 8, A_CH], BF16, **R))
                qev = st_a.enter_context(nc.sbuf_tensor("qev", [128, 4, A_CH], BF16, **R))
                kev = st_a.enter_context(nc.sbuf_tensor("kev", [128, 4, A_CH], BF16, **R))
                cosl = st_a.enter_context(nc.sbuf_tensor("cosl", [128, A_CH], BF16, **R))
                sinl = st_a.enter_context(nc.sbuf_tensor("sinl", [128, A_CH], BF16, **R))
                tstg = st_a.enter_context(nc.sbuf_tensor("tstg", [128, 2, A_CH], F32, **R))
                rt1 = st_a.enter_context(nc.sbuf_tensor("rt1", [128, A_CH], BF16, **R))
                rt2 = st_a.enter_context(nc.sbuf_tensor("rt2", [128, A_CH], BF16, **R))
                sqscr = st_a.enter_context(nc.sbuf_tensor("sqscr", [128, DM], BF16, **R))
                stt = st_a.enter_context(nc.sbuf_tensor("stt", [128, 32], F32, **R))
                p_t = st_a.enter_context(nc.psum_tensor("p_t", [128, 2, 1024], BF16))
                p_qkv = st_a.enter_context(nc.psum_tensor("p_qkv", [128, 4, 512], F32))

                with nc.Block() as blk:
                    for dsrc, wdst in ((d_wq, wq_sb), (d_wk, wk_sb), (d_wv, wv_sb)):
                        @blk.sync
                        def _(q, dsrc=dsrc):
                            q.wait_ge(sV.h, sV.n)
                            sd.inc(q.dma_start(
                                out=wstg[:, :, :],
                                in_=dsrc.rearrange("(dc p) f -> p dc f", p=128)), 16)
                        wdma = sd.n

                        @blk.vector
                        def _(v, wdst=wdst, wdma=wdma):
                            v.wait_ge(sd.h, wdma)
                            sV.inc(v.tensor_copy(wdst[:, :, :], wstg[:, :, :]))

                    @blk.scalar
                    def _(s):
                        s.preload_act_table(AF.Sqrt)

                with nc.Block() as blk:
                    v_chunk_done = 0
                    a_chunk_done = 0
                    ev_last = 0
                    for c in range(N_ACH):
                        T0 = c * A_CH
                        _vcd, _acd = v_chunk_done, a_chunk_done

                        @blk.sync
                        def _(q, T0=T0, _vcd=_vcd, _acd=_acd):
                            q.wait_ge(sV.h, _vcd)
                            q.wait_ge(sA.h, _acd)
                            sd.inc(q.dma_start(
                                out=xa[:, :, :],
                                in_=d_x[T0:T0 + A_CH, :].rearrange(
                                    "(tt p) d -> p tt d", p=128)), 16)
                            sd.inc(q.dma_start(out=tstg[:, 0, :],
                                               in_=d_cos[:, T0:T0 + A_CH]), 16)
                            sd.inc(q.dma_start(out=tstg[:, 1, :],
                                               in_=d_sin[:, T0:T0 + A_CH]), 16)
                        dma_x = sd.n

                        @blk.vector
                        def _(v, dma_x=dma_x):
                            v.wait_ge(sd.h, dma_x)
                            v.tensor_copy(cosl[:, :], tstg[:, 0, :])
                            sV.inc(v.tensor_copy(sinl[:, :], tstg[:, 1, :]))

                        for tt in range(4):
                            @blk.scalar
                            def _(s, tt=tt, dma_x=dma_x):
                                s.wait_ge(sd.h, dma_x)
                                sA.inc(s.activation(sqscr[:, :], xa[:, tt, :],
                                                    AF.Square, accum_out=stt[:, 4 + tt:5 + tt]))
                            sq_done = sA.n

                            @blk.vector
                            def _(v, tt=tt, dma_x=dma_x, sq_done=sq_done):
                                v.wait_ge(sd.h, dma_x)
                                def VF(instr):
                                    sV.inc(instr)
                                    v.wait_ge(sV.h, sV.n)
                                VF(v.reduce_sum(stt[:, tt:tt + 1], xa[:, tt, :], AX.X))
                                v.wait_ge(sA.h, sq_done)
                                VF(v.tensor_scalar_mul(stt[:, 24 + tt:25 + tt],
                                                       stt[:, tt:tt + 1], 1.0 / DM))
                                VF(v.tensor_scalar_mul(stt[:, 8 + tt:9 + tt],
                                                       stt[:, 4 + tt:5 + tt], 1.0 / DM))
                                VF(v.tensor_mul(stt[:, 12 + tt:13 + tt],
                                                stt[:, 24 + tt:25 + tt],
                                                stt[:, 24 + tt:25 + tt]))
                                VF(v.tensor_sub(stt[:, 16 + tt:17 + tt],
                                                stt[:, 8 + tt:9 + tt],
                                                stt[:, 12 + tt:13 + tt]))
                                if tt == 3:
                                    VF(v.tensor_single_scalar(stt[:, 20:24], stt[:, 16:20],
                                                              epsap[:, 0:1], ALU.add))
                        veps_done = sV.n

                        @blk.scalar
                        def _(s, veps_done=veps_done):
                            s.wait_ge(sV.h, veps_done)
                            sA.inc(s.activation(stt[:, 8:12], stt[:, 20:24], AF.Sqrt,
                                                bias=zeroap[:, 0:1]))
                        sqrt_done = sA.n

                        @blk.vector
                        def _(v, sqrt_done=sqrt_done):
                            v.wait_ge(sA.h, sqrt_done)
                            def VF(instr):
                                sV.inc(instr)
                                v.wait_ge(sV.h, sV.n)
                            VF(v.reciprocal(stt[:, 12:16], stt[:, 8:12]))
                            VF(v.tensor_mul(stt[:, 16:20], stt[:, 20:24], stt[:, 12:16]))
                            VF(v.tensor_add(stt[:, 12:16], stt[:, 16:20], stt[:, 8:12]))
                            VF(v.tensor_scalar_mul(stt[:, 16:20], stt[:, 12:16], 0.5))
                            VF(v.reciprocal(stt[:, 28:32], stt[:, 16:20]))
                            for tt in range(4):
                                instr = v.tensor_scalar(xn_tm[:, tt, :], xa[:, tt, :],
                                                        stt[:, 24 + tt:25 + tt],
                                                        stt[:, 28 + tt:29 + tt],
                                                        ALU.subtract, ALU.mult)
                                if tt == 3:
                                    sV.inc(instr)
                        ln_done = sV.n

                        tcopy0 = sV.n
                        for j in range(32):
                            tt, dc = j // 8, j % 8

                            @blk.tensor
                            def _(t, tt=tt, dc=dc, j=j, ln_done=ln_done, tcopy0=tcopy0):
                                t.wait_ge(sV.h, ln_done if j < 2 else tcopy0 + j - 1)
                                sT.inc(t.transpose(p_t[:, j % 2, 0:128],
                                                   xn_tm[:, tt, dc * 128:(dc + 1) * 128],
                                                   eyeb[:, :]))
                            tr_done = sT.n

                            @blk.vector
                            def _(v, tt=tt, dc=dc, j=j, tr_done=tr_done):
                                v.wait_ge(sT.h, tr_done)
                                sV.inc(v.tensor_copy(xnf[:, dc, tt * 128:(tt + 1) * 128],
                                                     p_t[:, j % 2, 0:128]))
                        xnf_done = sV.n

                        for wsb, evt, x1t, x2t, qscale in (
                                (wq_sb, qev, qx1, qx2, 0.125),
                                (wk_sb, kev, kx1, kx2, 1.0)):
                            _evl = ev_last

                            @blk.tensor
                            def _(t, wsb=wsb, xnf_done=xnf_done, _evl=_evl):
                                t.wait_ge(sV.h, xnf_done)
                                t.wait_ge(sA.h, _evl)
                                mm = None
                                for fc in range(4):
                                    for dc in range(8):
                                        mm = t.matmul(p_qkv[:, fc, :],
                                                      wsb[:, dc, fc * 128:(fc + 1) * 128],
                                                      xnf[:, dc, :],
                                                      start=(dc == 0), stop=(dc == 7))
                                sT.inc(mm)
                            mm_done = sT.n

                            @blk.scalar
                            def _(s, evt=evt, qscale=qscale, mm_done=mm_done):
                                s.wait_ge(sT.h, mm_done)
                                for fc in range(4):
                                    sA.inc(s.activation(evt[:, fc, :], p_qkv[:, fc, :],
                                                        AF.Copy, scale=qscale))
                            ev_last = sA.n
                            _evd = sA.n

                            @blk.vector
                            def _(v, evt=evt, x1t=x1t, x2t=x2t, T0=T0, _evd=_evd):
                                v.wait_ge(sA.h, _evd)
                                for g in range(2):
                                    v.tensor_mul(rt1[:, :], evt[:, 2 * g, :], cosl[:, :])
                                    v.tensor_mul(rt2[:, :], evt[:, 2 * g + 1, :], sinl[:, :])
                                    v.tensor_sub(x1t[:, g, T0:T0 + A_CH],
                                                 rt1[:, :], rt2[:, :])
                                    v.tensor_mul(rt1[:, :], evt[:, 2 * g + 1, :], cosl[:, :])
                                    v.tensor_mul(rt2[:, :], evt[:, 2 * g, :], sinl[:, :])
                                    sV.inc(v.tensor_add(x2t[:, g, T0:T0 + A_CH],
                                                        rt1[:, :], rt2[:, :]))

                        _evl = ev_last

                        @blk.tensor
                        def _(t, xnf_done=xnf_done, _evl=_evl):
                            t.wait_ge(sV.h, xnf_done)
                            t.wait_ge(sA.h, _evl)
                            mm = None
                            for tt in range(4):
                                for dc in range(8):
                                    mm = t.matmul(p_qkv[:, tt, :],
                                                  xnf[:, dc, tt * 128:(tt + 1) * 128],
                                                  wv_sb[:, dc, :],
                                                  start=(dc == 0), stop=(dc == 7))
                            sT.inc(mm)
                        vmm_done = sT.n

                        @blk.scalar
                        def _(s, c=c, vmm_done=vmm_done):
                            s.wait_ge(sT.h, vmm_done)
                            for tt in range(4):
                                sA.inc(s.activation(v_tm[:, 4 * c + tt, :],
                                                    p_qkv[:, tt, :], AF.Copy))
                        ev_last = sA.n
                        v_chunk_done = sV.n
                        a_chunk_done = sA.n

                def emit_bf16_tap(name, src_flat, cols):
                    dst = tapt(name, (128, cols))
                    if dst is None:
                        return
                    for off in range(0, cols, 4096):
                        w = min(4096, cols - off)
                        with nc.Block() as blk:
                            @blk.vector
                            def _(v, off=off, w=w):
                                v.tensor_copy(
                                    wstg.ap().rearrange("p a b -> p (a b)")[:, 0:w],
                                    src_flat[:, off:off + w])
                        with nc.Block() as blk:
                            @blk.sync
                            def _(q, off=off, w=w):
                                sd.inc(q.dma_start(
                                    out=dst[:, off:off + w],
                                    in_=wstg.ap().rearrange("p a b -> p (a b)")[:, 0:w]), 16)
                                q.wait_ge(sd.h, sd.n)

                emit_bf16_tap("xn7", xn_tm.ap().rearrange("p a b -> p (a b)"), 4 * DM)
                emit_bf16_tap("xnf7", xnf.ap().rearrange("p a b -> p (a b)"), 8 * A_CH)
                emit_bf16_tap("qx1", qx1.ap().rearrange("p g s -> p (g s)"), 2 * S)
                emit_bf16_tap("qx2", qx2.ap().rearrange("p g s -> p (g s)"), 2 * S)
                emit_bf16_tap("kx1", kx1.ap().rearrange("p g s -> p (g s)"), 2 * S)
                emit_bf16_tap("kx2", kx2.ap().rearrange("p g s -> p (g s)"), 2 * S)
                emit_bf16_tap("v_tm", v_tm.ap().rearrange("p a b -> p (a b)"), 32 * 512)

            DO_B = "B" in phases
            w1q = stk.enter_context(nc.sbuf_tensor("w1q", [128, 8, HID], FP8, side="left"))

            with ExitStack() as st_b:
                pT8 = st_b.enter_context(nc.sbuf_tensor("pT8", [128, 2, 2048], BF16, **R))
                zsb = st_b.enter_context(nc.sbuf_tensor("zsb", [64, 2, 512], F32, **R))
                rzsb = st_b.enter_context(nc.sbuf_tensor("rzsb", [64, 2, 512], F32, **R))
                of32 = st_b.enter_context(nc.sbuf_tensor("of32", [128, 4, 256], F32, **R))
                o_ep = st_b.enter_context(nc.sbuf_tensor("o_ep", [128, 4, 256], BF16, **R))
                aosb = st_b.enter_context(nc.sbuf_tensor("aosb", [128, 2, DM], F32, **R))
                wstgb = st_b.enter_context(nc.sbuf_tensor("wstgb", [128, 2, 2048], F32, **R))
                wqscr = st_b.enter_context(nc.sbuf_tensor("wqscr", [128, 2048], F32, **R))
                sr1 = st_b.enter_context(nc.sbuf_tensor("sr1", [1, 132], F32, **R))
                sttb = st_b.enter_context(nc.sbuf_tensor("sttb", [128, 8], F32, **R))
                p_big = st_b.enter_context(nc.psum_tensor("p_big", [128, 2048], F32))
                p_o = st_b.enter_context(nc.psum_tensor("p_o", [128, 1024], F32))
                p_z = st_b.enter_context(nc.psum_tensor("p_z", [128, 512], F32))
                p_rz = st_b.enter_context(nc.psum_tensor("p_rz", [128, 512], F32))

                def hcol(h):
                    return 512 * (h % 4) + 256 * (h // 4)

                with nc.Block() as blk:
                  if DO_B:
                    @blk.scalar
                    def _(s):
                        s.preload_act_table(AF.Exp)

                    @blk.sync
                    def _(q):
                        sd.inc(q.dma_start(
                            out=wstgb.ap().rearrange("p a b -> p (a b)").rearrange(
                                "p (pc d) -> p pc d", d=DM),
                            in_=d_wo.rearrange("(pc p) d -> p pc d", p=128)), 16)
                        sd.inc(q.dma_start(out=d_ccin[0:1, :], in_=zerof[0:1, :]), 16)
                        sd.inc(q.dma_start(out=d_ccin[2 * DR - 1:2 * DR, :],
                                           in_=zerof[0:1, :]), 16)
                    wo_dma = sd.n

                    @blk.vector
                    def _(v):
                        v.wait_ge(sd.h, wo_dma)
                        sV.inc(v.tensor_copy(
                            wo_sb[:, :, :],
                            wstgb.ap().rearrange("p a b -> p (a b)").rearrange(
                                "p (pc d) -> p pc d", d=DM)))

                with nc.Block() as blk:
                  if DO_B:
                    st_state = {
                        "w1_stage_free_v": sV.n,
                        "w1_stage_free_g": sG.n,
                        "prev_ao": 0,
                        "prev_oc": 0,
                        "prev_zr": 0,
                        "prev_om": 0,
                        "prev_avz": 0,
                        "prev_ccin": 0,
                        "pz_scrib_v": 0,
                    }
                    exp_at = {}
                    exp_half = {}
                    mask_at = {}
                    avz_at = {}

                    def emit_w1_mean(dc):
                        _vf = st_state["w1_stage_free_v"]
                        _gf = st_state["w1_stage_free_g"]

                        @blk.sync
                        def _(q, dc=dc, _vf=_vf):
                            q.wait_ge(sV.h, _vf)
                            sd.inc(q.dma_start(
                                out=wstgb[:, 0, :],
                                in_=d_w1t[128 * dc:128 * (dc + 1), 0:2048]), 16)
                            sd.inc(q.dma_start(
                                out=wstgb[:, 1, :],
                                in_=d_w1t[128 * dc:128 * (dc + 1), 2048:4096]), 16)
                        dmac = sd.n

                        @blk.vector
                        def _(v, dc=dc, dmac=dmac):
                            v.wait_ge(sd.h, dmac)
                            def VF(instr):
                                sV.inc(instr)
                                v.wait_ge(sV.h, sV.n)
                            VF(v.tensor_reduce(sttb[:, 0:1], wstgb[:, 0, :], AX.X,
                                               ALU.add, apply_absolute_value=True))
                            VF(v.tensor_reduce(sttb[:, 1:2], wstgb[:, 1, :], AX.X,
                                               ALU.add, apply_absolute_value=True))
                            VF(v.tensor_add(sttb[:, 2:3], sttb[:, 0:1], sttb[:, 1:2]))
                            cp = 3 + (dc % 2)
                            cq = 3 + ((dc + 1) % 2)
                            if dc == 0:
                                sV.inc(v.tensor_copy(sttb[:, cp:cp + 1], sttb[:, 2:3]))
                            else:
                                sV.inc(v.tensor_add(sttb[:, cp:cp + 1],
                                                    sttb[:, cq:cq + 1], sttb[:, 2:3]))
                            v.wait_ge(sV.h, sV.n)
                        st_state["w1_stage_free_v"] = sV.n

                    def emit_w1_scale():
                        prev = sV.n

                        @blk.sync
                        def _(q, prev=prev):
                            q.wait_ge(sV.h, prev)
                            sd.inc(q.dma_start(out=d_w1s.ap(), in_=sttb[:, 4:5]), 16)
                            q.wait_ge(sd.h, sd.n)
                            sd.inc(q.dma_start(out=sr1[0:1, 0:128], in_=d_w1s.ap()), 16)
                        rowd = sd.n

                        @blk.vector
                        def _(v, rowd=rowd):
                            v.wait_ge(sd.h, rowd)
                            def VF(instr):
                                sV.inc(instr)
                                v.wait_ge(sV.h, sV.n)
                            VF(v.reduce_sum(sr1[0:1, 128:129], sr1[0:1, 0:128], AX.X))
                            VF(v.tensor_scalar(sr1[0:1, 129:130], sr1[0:1, 128:129],
                                               1.0 / (DM * HID), 1e-5, ALU.mult, ALU.max))
                            VF(v.reciprocal(sr1[0:1, 130:131], sr1[0:1, 129:130]))
                        rcd = sV.n

                        @blk.sync
                        def _(q, rcd=rcd):
                            q.wait_ge(sV.h, rcd)
                            sd.inc(q.dma_start(out=d_w1v.ap(), in_=sr1[0:1, 129:131]), 16)
                            q.wait_ge(sd.h, sd.n)
                            sd.inc(q.dma_start(
                                out=s1bc[:, :],
                                in_=bass.AP(d_w1v.ap().tensor, 1, [[0, 128], [1, 1]])), 16)
                            sd.inc(q.dma_start(
                                out=scal[:, 0:1],
                                in_=bass.AP(d_w1v.ap().tensor, 0, [[0, 128], [1, 1]])), 16)
                        bcdma = sd.n

                        @blk.vector
                        def _(v, bcdma=bcdma):
                            v.wait_ge(sd.h, bcdma)
                            sV.inc(v.tensor_scalar_mul(cwq[:, :], cwp[:, :],
                                                       scal[:, 0:1]))
                            v.wait_ge(sV.h, sV.n)
                        st_state["w1_stage_free_v"] = sV.n

                    def emit_w1_quant(dc):
                        _vf = st_state["w1_stage_free_v"]
                        _gf = st_state["w1_stage_free_g"]

                        @blk.sync
                        def _(q, dc=dc, _vf=_vf):
                            q.wait_ge(sV.h, _vf)
                            sd.inc(q.dma_start(
                                out=wstgb[:, 0, :],
                                in_=d_w1t[128 * dc:128 * (dc + 1), 0:2048]), 16)
                            sd.inc(q.dma_start(
                                out=wstgb[:, 1, :],
                                in_=d_w1t[128 * dc:128 * (dc + 1), 2048:4096]), 16)
                        dmac = sd.n

                        for half in range(2):
                            @blk.vector
                            def _(v, dc=dc, dmac=dmac, half=half):
                                v.wait_ge(sd.h, dmac)
                                v.tensor_scalar(wqscr[:, :],
                                                wstgb[:, half, :],
                                                s1bc[:, 0:1], MAGIC,
                                                ALU.mult, ALU.add)
                                v.tensor_scalar(wstgb[:, half, :], wqscr[:, :],
                                                MAGIC, -1.0, ALU.subtract, ALU.max)
                                sV.inc(v.tensor_scalar_min(
                                    w1q[:, dc, 2048 * half:2048 * (half + 1)],
                                    wstgb[:, half, :], 1.0))
                        st_state["w1_stage_free_v"] = sV.n

                    w1_jobs = ([("m", dc) for dc in range(8)] + [("s", 0)] +
                               [("q", dc) for dc in range(8)])
                    w1_ji = [0]

                    def emit_w1_some(n_jobs):
                        for _ in range(n_jobs):
                            if w1_ji[0] >= len(w1_jobs):
                                return
                            kind, dc = w1_jobs[w1_ji[0]]
                            w1_ji[0] += 1
                            if kind == "m":
                                emit_w1_mean(dc)
                            elif kind == "s":
                                emit_w1_scale()
                            else:
                                emit_w1_quant(dc)

                    for qb in range(N_QB):
                        q0 = qb * QB
                        KT = 2 * qb + 2
                        _poc, _pzr = st_state["prev_oc"], st_state["prev_zr"]

                        @blk.tensor
                        def _(t, _poc=_poc, _pzr=_pzr):
                            t.wait_ge(sOC.h, _poc)
                            t.wait_ge(sZR.h, _pzr)
                            for half in range(2):
                                t.matmul(p_o[:, 512 * half:512 * half + 512],
                                         zerob[0:1, 0:128], zerob[0:1, 0:512],
                                         start=True, stop=True)
                            sT.inc(t.matmul(p_z[:, :], zerob[0:1, 0:128],
                                            zerob[0:1, 0:512], start=True, stop=True))

                        def emit_scores_exp(kt):
                            _pao = st_state["prev_ao"]
                            _pavz = st_state["prev_avz"]
                            for half in range(2):
                                js = (0, 1) if half == 0 else (2, 3)

                                @blk.tensor
                                def _(t, kt=kt, qb=qb, q0=q0, _pao=_pao,
                                      half=half, js=js):
                                    k0 = kt * 128
                                    if kt >= 1:
                                        t.wait_ge(sE.h, exp_half[(qb, kt - 1, half)])
                                    elif half == 0:
                                        t.wait_ge(sAO.h, _pao)
                                    last = None
                                    for g in range(2):
                                        for j in js:
                                            h = 4 * g + j
                                            sl = slice(32 * j, 32 * j + 32)
                                            co = hcol(h)
                                            t.matmul(p_big[:, co:co + 256],
                                                     kx1[sl, g, k0:k0 + 128],
                                                     qx1[sl, g, q0:q0 + QB],
                                                     start=True, stop=False,
                                                     tile_position=(32 * j, 0))
                                            last = t.matmul(p_big[:, co:co + 256],
                                                            kx2[sl, g, k0:k0 + 128],
                                                            qx2[sl, g, q0:q0 + QB],
                                                            start=False, stop=True,
                                                            tile_position=(32 * j, 0))
                                    sT.inc(last)
                                sc_cnt = sT.n

                                @blk.scalar
                                def _(s, kt=kt, qb=qb, sc_cnt=sc_cnt, _pavz=_pavz,
                                      half=half):
                                    s.wait_ge(sT.h, sc_cnt)
                                    if half == 0:
                                        if kt >= 2:
                                            s.wait_ge(sAV.h, avz_at[(qb, kt - 2)])
                                        elif _pavz:
                                            s.wait_ge(sAV.h, _pavz)
                                    c0 = 1024 * half
                                    sE.inc(s.activation(
                                        pT8[:, kt % 2, c0:c0 + 1024],
                                        p_big[:, c0:c0 + 1024], AF.Exp))
                                exp_half[(qb, kt, half)] = sE.n
                            exp_at[(qb, kt)] = sE.n

                        def emit_mask(kt, d):
                            @blk.gpsimd
                            def _(g, kt=kt, d=d, qb=qb):
                                g.wait_ge(sE.h, exp_at[(qb, kt)])
                                sM.inc(g.affine_select(
                                    pT8.ap()[:, kt % 2, :].rearrange(
                                        "p (h j) -> p h j", j=256),
                                    pT8.ap()[:, kt % 2, :].rearrange(
                                        "p (h j) -> p h j", j=256),
                                    [[0, 8], [1, 256]],
                                    ALU.is_ge, 0.0,
                                    base=-128 * d,
                                    channel_multiplier=-1))
                            mask_at[(qb, kt)] = sM.n

                        def emit_avz(kt):
                            @blk.tensor
                            def _(t, kt=kt, qb=qb, KT=KT):
                                if kt >= KT - 2:
                                    t.wait_ge(sM.h, mask_at[(qb, kt)])
                                else:
                                    t.wait_ge(sE.h, exp_at[(qb, kt)])
                                lastkt = (kt == KT - 1)
                                buf = kt % 2
                                for p in range(4):
                                    for sub in range(2):
                                        h = 2 * p + sub
                                        t.matmul(
                                            p_o[64 * sub:64 * sub + 64,
                                                256 * p:256 * p + 256],
                                            v_tm[:, kt, 64 * h:64 * h + 64],
                                            pT8[:, buf, hcol(h):hcol(h) + 256],
                                            start=False, stop=lastkt,
                                            tile_position=(0, 64 * sub))
                                last = None
                                for gp in range(4):
                                    last = t.matmul(
                                        p_z[32 * gp:32 * gp + 1, :],
                                        onesb[:, 0:1],
                                        pT8[:, buf, 512 * gp:512 * gp + 512],
                                        start=False, stop=lastkt,
                                        tile_position=(0, 32 * gp))
                                sAV.inc(last)
                            avz_at[(qb, kt)] = sAV.n

                        for kt in range(KT):
                            emit_scores_exp(kt)
                            if kt >= KT - 2:
                                emit_mask(kt, kt - (KT - 2))
                            if kt >= 1:
                                emit_avz(kt - 1)
                        emit_avz(KT - 1)
                        avz_all = sAV.n
                        st_state["prev_avz"] = avz_all

                        @blk.vector
                        def _(v, avz_all=avz_all):
                            v.wait_ge(sAV.h, avz_all)
                            v.tensor_copy(zsb[:, 0, :], p_z[0:64, :])
                            sV.inc(v.tensor_copy(zsb[:, 1, :], p_z[64:128, :]))
                            v.wait_ge(sV.h, sV.n)
                            v.reciprocal(rzsb[:, 0, :], zsb[:, 0, :])
                            sV.inc(v.reciprocal(rzsb[:, 1, :], zsb[:, 1, :]))
                            v.wait_ge(sV.h, sV.n)
                            sV.inc(v.tensor_copy(sttb[0:1, 0:1], rzsb[0:1, 1, 0:8][0:1, 0:1]))
                            v.wait_ge(sV.h, sV.n)
                            sZR.inc(v.tensor_copy(sttb[0:1, 1:2], rzsb[0:1, 0, 0:8][0:1, 0:1]))
                        zr_done = sZR.n
                        st_state["prev_zr"] = sZR.n

                        _pom0 = st_state["prev_om"]

                        @blk.scalar
                        def _(s, avz_all=avz_all, _pom0=_pom0):
                            s.wait_ge(sAV.h, avz_all)
                            s.wait_ge(sOM.h, _pom0)
                            for p in range(4):
                                sOC.inc(s.activation(of32[:, p, :],
                                                     p_o[:, 256 * p:256 * p + 256],
                                                     AF.Copy))
                        oc_done = sOC.n
                        st_state["prev_oc"] = sOC.n

                        for half in range(2):
                            _pom = st_state["prev_om"]

                            @blk.tensor
                            def _(t, half=half, zr_done=zr_done, _pom=_pom):
                                t.wait_ge(sZR.h, zr_done)
                                t.wait_ge(sOM.h, _pom)
                                last = None
                                for pp in range(2):
                                    p = 2 * half + pp
                                    for sub in range(2):
                                        h = 2 * p + sub
                                        j, g = h % 4, h // 4
                                        tile = j // 2
                                        pr = 32 * (j % 2)
                                        last = t.matmul(
                                            p_rz[64 * sub:64 * sub + 64,
                                                 256 * pp:256 * pp + 256],
                                            onesf[pr:pr + 1, 0:64],
                                            rzsb[pr:pr + 1, tile,
                                                 256 * g:256 * g + 256],
                                            start=True, stop=True,
                                            tile_position=(pr, 64 * sub))
                                sBC.inc(last)
                            bc_done = sBC.n

                            @blk.vector
                            def _(v, half=half, bc_done=bc_done, oc_done=oc_done):
                                v.wait_ge(sBC.h, bc_done)
                                v.wait_ge(sOC.h, oc_done)
                                for pp in range(2):
                                    p = 2 * half + pp
                                    sOM.inc(v.tensor_mul(
                                        o_ep[:, p, :], of32[:, p, :],
                                        p_rz[:, 256 * pp:256 * pp + 256]))
                            st_state["prev_om"] = sOM.n
                        om_done = sOM.n

                        @blk.tensor
                        def _(t, qb=qb, om_done=om_done, KT=KT):
                            t.wait_ge(sOM.h, om_done)
                            t.wait_ge(sE.h, exp_at[(qb, KT - 1)])
                            for tc in range(2):
                                for n in range(2):
                                    mm = None
                                    for p in range(4):
                                        mm = t.matmul(
                                            p_big[:, 1024 * tc + 512 * n:
                                                  1024 * tc + 512 * n + 512],
                                            o_ep[:, p, 128 * tc:128 * tc + 128],
                                            wo_sb[:, p, 512 * n:512 * n + 512],
                                            start=(p == 0), stop=(p == 3))
                                    sWO.inc(mm)
                        wo_done = sWO.n

                        _pcc = st_state["prev_ccin"]

                        @blk.vector
                        def _(v, wo_done=wo_done, _pcc=_pcc):
                            v.wait_ge(sWO.h, wo_done)
                            v.wait_ge(sd.h, _pcc)
                            for tc in range(2):
                                sAO.inc(v.tensor_copy(
                                    aosb[:, tc, :],
                                    p_big[:, 1024 * tc:1024 * tc + 1024]))
                        ao_done = sAO.n
                        st_state["prev_ao"] = ao_done

                        @blk.sync
                        def _(q, q0=q0, ao_done=ao_done):
                            q.wait_ge(sAO.h, ao_done)
                            for tc in range(2):
                                t0 = q0 + 128 * tc
                                if t0 <= OWN:
                                    n_r = min(128, OWN + 1 - t0)
                                    sd.inc(q.dma_start(
                                        out=d_ccin[t0 + 1:t0 + 1 + n_r, :],
                                        in_=aosb[0:n_r, tc, :]), 16)
                                lo = max(t0, OWN - 1)
                                if lo < t0 + 128:
                                    off = lo - t0
                                    n_r = 128 - off
                                    r0 = DR + lo - (OWN - 1)
                                    sd.inc(q.dma_start(
                                        out=d_ccin[r0:r0 + n_r, :],
                                        in_=aosb[off:off + n_r, tc, :]), 16)
                        st_state["prev_ccin"] = sd.n

                        emit_w1_some(2 if qb < 8 else 1)

                    if "o_last" in taps:
                        dst = tapt("o_last", (128, 4 * 256))
                        @blk.vector
                        def _(v):
                            v.tensor_copy(wstgb.ap().rearrange("p a b -> p (a b)")[:, 0:1024],
                                          o_ep.ap().rearrange("p a b -> p (a b)"))
                        @blk.sync
                        def _(q):
                            q.wait_ge(sV.h, sV.n)
                            sd.inc(q.dma_start(
                                out=dst,
                                in_=wstgb.ap().rearrange("p a b -> p (a b)")[:, 0:1024]), 16)

                    ccin_dma = sd.n

                    @blk.gpsimd
                    def _(g, ccin_dma=ccin_dma):
                        g.wait_ge(sd.h, ccin_dma)
                        scc.inc(g.collective_compute(
                            "ReduceScatter", ALU.add,
                            replica_groups=[[0, 1], [2, 3], [4, 5], [6, 7]],
                            ins=[d_ccin.ap().opt()], outs=[d_ccout.ap().opt()]))

        DO_D = "D" in phases
        w2q = stk.enter_context(nc.sbuf_tensor("w2q", [128, 32, DM], FP8, side="left"))

        with ExitStack() as st_d:
            scrd = st_d.enter_context(nc.sbuf_tensor("scrd", [128, 3, DM], F32, **R))
            sttd = st_d.enter_context(nc.sbuf_tensor("sttd", [128, 12], F32, **R))
            srd = st_d.enter_context(nc.sbuf_tensor("srd", [1, 132], F32, **R))
            p_d = st_d.enter_context(nc.psum_tensor("p_d", [128, 512], F32))

            with ExitStack() as st_d0:
                wstgd = st_d0.enter_context(nc.sbuf_tensor("wstgd", [128, 4, DM], F32, **R))
                wdscr = st_d0.enter_context(nc.sbuf_tensor("wdscr", [128, 2, DM], F32, **R))
                dst = st_d0.enter_context(nc.sbuf_tensor("dst", [128, 6, DM], F32, **R))

                with nc.Block() as blk:
                  if DO_D:
                    @blk.scalar
                    def _(s):
                        s.preload_act_table(AF.Sqrt)

                    @blk.sync
                    def _(q):
                        q.wait_ge(scc.h, 1)

                    @blk.vector
                    def _(v):
                        v.wait_ge(scc.h, 1)

                    @blk.gpsimd
                    def _(g):
                        g.wait_ge(scc.h, 1)

                    @blk.tensor
                    def _(t):
                        t.wait_ge(scc.h, 1)

                    mean_vdone = [0] * 32
                    pre_vdone = [0] * N_STAT

                    def emit_mean_chunk(hc):
                        _vf = mean_vdone[hc - 4] if hc >= 4 else 0

                        @blk.gpsimd
                        def _(g, hc=hc, _vf=_vf):
                            g.wait_ge(scc.h, 1)
                            if _vf:
                                g.wait_ge(sV.h, _vf)
                            sd2.inc(g.dma_start(
                                out=wstgd[:, hc % 4, :],
                                in_=d_w2t[128 * hc:128 * (hc + 1), :]), 16)
                        dmac = sd2.n

                        @blk.vector
                        def _(v, hc=hc, dmac=dmac):
                            v.wait_ge(sd2.h, dmac)
                            sV.inc(v.tensor_reduce(sttd[:, 8:9], wstgd[:, hc % 4, :],
                                                   AX.X, ALU.add,
                                                   apply_absolute_value=True))
                            v.wait_ge(sV.h, sV.n)
                            cp = 9 + (hc % 2)
                            cq = 9 + ((hc + 1) % 2)
                            if hc == 0:
                                sV.inc(v.tensor_copy(sttd[:, cp:cp + 1], sttd[:, 8:9]))
                            else:
                                sV.inc(v.tensor_add(sttd[:, cp:cp + 1],
                                                    sttd[:, cq:cq + 1], sttd[:, 8:9]))
                            v.wait_ge(sV.h, sV.n)
                        mean_vdone[hc] = sV.n

                    def emit_pre_tile(jt):
                        r0 = 128 * jt
                        cnt = 128 if jt < 16 else DR - 2048
                        sl = 3 * (jt % 2)
                        _df = pre_vdone[jt - 2] if jt >= 2 else 0

                        @blk.sync
                        def _(q, r0=r0, cnt=cnt, sl=sl, _df=_df):
                            q.wait_ge(scc.h, 1)
                            if _df:
                                q.wait_ge(sV.h, _df)
                            sd.inc(q.dma_start(out=dst[0:cnt, sl + 0, :],
                                               in_=d_x0s[r0:r0 + cnt, :]), 16)
                            sd.inc(q.dma_start(out=dst[0:cnt, sl + 1, :],
                                               in_=d_ccout[r0:r0 + cnt, :]), 16)
                        dmac = sd.n

                        @blk.vector
                        def _(v, cnt=cnt, sl=sl, dmac=dmac):
                            v.wait_ge(sd.h, dmac)
                            sV.inc(v.tensor_add(dst[0:cnt, sl + 2, :],
                                                dst[0:cnt, sl + 0, :],
                                                dst[0:cnt, sl + 1, :]))
                        x1_done = sV.n

                        @blk.scalar
                        def _(s, cnt=cnt, sl=sl, jt=jt, x1_done=x1_done):
                            s.wait_ge(sV.h, x1_done)
                            sA.inc(s.activation(dst[0:cnt, sl + 1, :],
                                                dst[0:cnt, sl + 2, :],
                                                AF.Square,
                                                accum_out=sttd[0:cnt, 1 + 4 * (jt % 2):
                                                               2 + 4 * (jt % 2)]))
                        sq_done = sA.n

                        @blk.vector
                        def _(v, jt=jt, cnt=cnt, sl=sl, sq_done=sq_done):
                            so = 4 * (jt % 2)
                            def VF(instr):
                                sV.inc(instr)
                                v.wait_ge(sV.h, sV.n)
                            VF(v.reduce_sum(sttd[0:cnt, so:so + 1],
                                            dst[0:cnt, sl + 2, :], AX.X))
                            v.wait_ge(sA.h, sq_done)
                            VF(v.tensor_scalar_mul(m_arr[0:cnt, jt:jt + 1],
                                                   sttd[0:cnt, so:so + 1], 1.0 / DM))
                            VF(v.tensor_scalar_mul(sttd[0:cnt, so + 2:so + 3],
                                                   sttd[0:cnt, so + 1:so + 2],
                                                   1.0 / DM))
                            VF(v.tensor_mul(sttd[0:cnt, so + 3:so + 4],
                                            m_arr[0:cnt, jt:jt + 1],
                                            m_arr[0:cnt, jt:jt + 1]))
                            VF(v.tensor_sub(var_arr[0:cnt, jt:jt + 1],
                                            sttd[0:cnt, so + 2:so + 3],
                                            sttd[0:cnt, so + 3:so + 4]))
                            VF(v.tensor_single_scalar(dst[0:cnt, sl + 0, :],
                                                      dst[0:cnt, sl + 2, :],
                                                      m_arr[0:cnt, jt:jt + 1],
                                                      ALU.subtract))
                            sV.inc(v.reduce_max(am_arr[0:cnt, jt:jt + 1],
                                                dst[0:cnt, sl + 0, :],
                                                AX.X, apply_absolute_value=True))
                        pre_vdone[jt] = sV.n

                    mi = [0]
                    pi = [0]
                    while mi[0] < 32 or pi[0] < N_STAT:
                        for _ in range(2):
                            if mi[0] < 32:
                                emit_mean_chunk(mi[0]); mi[0] += 1
                        if pi[0] < N_STAT:
                            emit_pre_tile(pi[0]); pi[0] += 1
                    stats_done = pre_vdone[N_STAT - 1]
                    prev = mean_vdone[31]

                    @blk.sync
                    def _(q, prev=prev):
                        q.wait_ge(sV.h, prev)
                        sd.inc(q.dma_start(out=d_w2s.ap(), in_=sttd[:, 10:11]), 16)
                        q.wait_ge(sd.h, sd.n)
                        sd.inc(q.dma_start(out=srd[0:1, 0:128], in_=d_w2s.ap()), 16)
                    rowd = sd.n

                    @blk.vector
                    def _(v, rowd=rowd):
                        v.wait_ge(sd.h, rowd)
                        def VF(instr):
                            sV.inc(instr)
                            v.wait_ge(sV.h, sV.n)
                        VF(v.reduce_sum(srd[0:1, 128:129], srd[0:1, 0:128], AX.X))
                        VF(v.tensor_scalar(srd[0:1, 129:130], srd[0:1, 128:129],
                                           1.0 / (DM * HID), 1e-5, ALU.mult, ALU.max))
                        VF(v.reciprocal(srd[0:1, 130:131], srd[0:1, 129:130]))
                    rcd = sV.n

                    @blk.sync
                    def _(q, rcd=rcd):
                        q.wait_ge(sV.h, rcd)
                        sd.inc(q.dma_start(out=d_w2v.ap(), in_=srd[0:1, 129:131]), 16)
                        q.wait_ge(sd.h, sd.n)
                        sd.inc(q.dma_start(
                            out=s2bc[:, :],
                            in_=bass.AP(d_w2v.ap().tensor, 1, [[0, 128], [1, 1]])), 16)
                        sd.inc(q.dma_start(
                            out=scal[:, 1:2],
                            in_=bass.AP(d_w2v.ap().tensor, 0, [[0, 128], [1, 1]])), 16)
                    bcdma = sd.n

                    @blk.vector
                    def _(v, bcdma=bcdma):
                        v.wait_ge(sd.h, bcdma)
                        sV.inc(v.tensor_scalar_mul(r127sw2[:, :], scal[:, 1:2],
                                                   1.0 / 127.0))
                        v.wait_ge(sV.h, sV.n)
                    s2_done = sV.n

                    quant_vdone = [0] * 32
                    for hc in range(32):
                        _gf = quant_vdone[hc - 4] if hc >= 4 else 0

                        @blk.gpsimd
                        def _(g, hc=hc, _gf=_gf, s2_done=s2_done):
                            if _gf:
                                g.wait_ge(sV.h, _gf)
                            if hc == 0:
                                g.wait_ge(sV.h, s2_done)
                            sd2.inc(g.dma_start(
                                out=wstgd[:, hc % 4, :],
                                in_=d_w2t[128 * hc:128 * (hc + 1), :]), 16)
                        dmac = sd2.n

                        @blk.vector
                        def _(v, hc=hc, dmac=dmac, s2_done=s2_done):
                            v.wait_ge(sd2.h, dmac)
                            if hc == 0:
                                v.wait_ge(sV.h, s2_done)
                            v.tensor_scalar(wdscr[:, hc % 2, :], wstgd[:, hc % 4, :],
                                            s2bc[:, 0:1], MAGIC,
                                            ALU.mult, ALU.add)
                            v.tensor_scalar(wstgd[:, hc % 4, :], wdscr[:, hc % 2, :],
                                            MAGIC, -1.0, ALU.subtract, ALU.max)
                            sV.inc(v.tensor_scalar_min(w2q[:, hc, :],
                                                       wstgd[:, hc % 4, :], 1.0))
                        quant_vdone[hc] = sV.n

                    @blk.vector
                    def _(v, stats_done=stats_done):
                        v.wait_ge(sV.h, stats_done)
                        sV.inc(v.tensor_single_scalar(ds17a[:, :], var_arr[:, :],
                                                      epsap[:, 0:1], ALU.add))
                    veps_d = sV.n

                    @blk.scalar
                    def _(s, veps_d=veps_d):
                        s.wait_ge(sV.h, veps_d)
                        sA.inc(s.activation(ds17b[:, :], ds17a[:, :], AF.Sqrt,
                                            bias=zeroap[:, 0:1]))
                    sdn = sA.n

                    @blk.vector
                    def _(v, sdn=sdn):
                        v.wait_ge(sA.h, sdn)
                        def VF(instr):
                            sV.inc(instr)
                            v.wait_ge(sV.h, sV.n)
                        VF(v.reciprocal(ds17c[:, :], ds17b[:, :]))
                        VF(v.tensor_mul(var_arr[:, :], ds17a[:, :], ds17c[:, :]))
                        VF(v.tensor_add(ds17c[:, :], var_arr[:, :], ds17b[:, :]))
                        VF(v.tensor_scalar_mul(ds17b[:, :], ds17c[:, :], 0.5))
                        VF(v.reciprocal(ds17c[:, :], ds17b[:, :]))
                        VF(v.tensor_mul(ds17a[:, :], am_arr[:, :], ds17c[:, :]))
                        VF(v.tensor_scalar_max(am_arr[:, :], ds17a[:, :], 1e-5))
                        VF(v.tensor_scalar_mul(isx_arr[:, :], am_arr[:, :], 1.0 / 127.0))
                        VF(v.reciprocal(ds17b[:, :], isx_arr[:, :]))
                        sV.inc(v.tensor_mul(rsx_arr[:, :], ds17b[:, :], ds17c[:, :]))

            if "ccout" in taps:
                dst = tapt("ccout", (DR, DM))
                with nc.Block() as blk:
                    @blk.sync
                    def _(q):
                        q.wait_ge(scc.h, 1)
                        sd.inc(q.dma_start(out=dst, in_=d_ccout.ap()), 16)
                        q.wait_ge(sd.h, sd.n)

            with ExitStack() as st_d1:
                ynq_tm = st_d1.enter_context(nc.sbuf_tensor("ynq_tm", [128, 2, DM], BF16, **R))
                ynqT = st_d1.enter_context(nc.sbuf_tensor("ynqT", [128, 8, 258], BF16, **R))
                hb0 = st_d1.enter_context(nc.sbuf_tensor("hb0", [128, 32, 258], BF16, **R))
                hb1 = st_d1.enter_context(nc.sbuf_tensor("hb1", [128, 32, 258], BF16, **R))
                u_sb = st_d1.enter_context(nc.sbuf_tensor("u_sb", [128, 32, 128], F32, **R))
                uq = st_d1.enter_context(nc.sbuf_tensor("uq", [128, 32, 128], BF16, **R))
                ct1 = st_d1.enter_context(nc.sbuf_tensor("ct1", [128, 8, 128], F32, **R))
                ct2 = st_d1.enter_context(nc.sbuf_tensor("ct2", [128, 8, 128], F32, **R))
                ct3 = st_d1.enter_context(nc.sbuf_tensor("ct3", [128, 8, 128], F32, **R))
                mm1s = st_d1.enter_context(nc.sbuf_tensor("mm1s", [128, 128], F32, **R))
                amu = st_d1.enter_context(nc.sbuf_tensor("amu", [128, 2], F32, **R))
                fsc = st_d1.enter_context(nc.sbuf_tensor("fsc", [128, 16], F32, **R))
                beta = st_d1.enter_context(nc.sbuf_tensor("beta", [128, 2], F32, **R))
                shv = st_d1.enter_context(nc.sbuf_tensor("shv", [128, 2], F32, **R))
                shrow = st_d1.enter_context(nc.sbuf_tensor("shrow", [1, 128], F32, **R))
                shbc = st_d1.enter_context(nc.sbuf_tensor("shbc", [128, 128], F32, **R))
                xres = st_d1.enter_context(nc.sbuf_tensor("xres", [128, 2, DM], F32, **R))
                ares = st_d1.enter_context(nc.sbuf_tensor("ares", [128, 2, DM], F32, **R))
                outsb = st_d1.enter_context(nc.sbuf_tensor("outsb", [128, 2, DM], F32, **R))
                p_t2 = st_d1.enter_context(nc.psum_tensor("p_t2", [128, 2, 1024], BF16))
                p_m1 = st_d1.enter_context(nc.psum_tensor("p_m1", [128, 2, 512], F32))
                p_y = st_d1.enter_context(nc.psum_tensor("p_y", [128, 1024], F32))
                p_sh = st_d1.enter_context(nc.psum_tensor("p_sh", [128, 512], F32))

                hbs = [hb0, hb1]

                with nc.Block() as blk:
                  if DO_D:
                    @blk.scalar
                    def _(s):
                        s.preload_act_table(AF.Silu)

                with nc.Block() as blk:
                  if DO_D:
                    state = {
                        "stg_free": 0,
                        "ynqT_free": 0,
                        "ynqT_done": 0,
                        "pt2_copy": {},
                        "hb_conv_done": {},
                        "hb_evac": {},
                        "halo_done": {},
                        "u_free": 0,
                        "uq_free": 0,
                        "py_free": 0,
                        "out_dma": 0,
                        "res_free": 0,
                        "round_done": 0,
                    }
                    w2q_ready = sV.n

                    class ConvEmitter:
                        def __init__(self, ob, sb, hd_v, hd_a):
                            self.hb = hbs[ob % 2]
                            self.c0 = 128 * sb
                            self.hd_v, self.hd_a = hd_v, hd_a
                            self.v_done = 0
                            self.g_done = {}
                            self.silu = {}
                            self.rd = state["round_done"]
                            self.uf = state["u_free"]
                            self.silu_done = 0

                        def emit_batch(self, bi):
                            hcs = range(8 * bi, 8 * bi + 8)
                            hb, c0 = self.hb, self.c0
                            _pv = self.v_done
                            _hv, _ha, _rd = self.hd_v, self.hd_a, self.rd

                            @blk.scalar
                            def _(s, hcs=hcs, _pv=_pv, bi=bi, c0=c0, hb=hb,
                                  _hv=_hv, _ha=_ha, _rd=_rd):
                                if bi == 0:
                                    s.wait_ge(sV.h, _hv)
                                    if _ha:
                                        s.wait_ge(sA.h, _ha)
                                    if _rd:
                                        s.wait_ge(sV.h, _rd)
                                if _pv:
                                    s.wait_ge(sV.h, _pv)
                                for hc in hcs:
                                    sA.inc(s.activation(
                                        ct1[:, hc % 8, :],
                                        hb[:, hc, c0 + 2:c0 + 130],
                                        AF.Identity,
                                        bias=cbp[:, hc:hc + 1],
                                        scale=cwq[:, 3 * hc + 2:3 * hc + 3]))
                            a_done = sA.n
                            self.emit_silu(bi - 1)

                            @blk.vector
                            def _(v, hcs=hcs, a_done=a_done, c0=c0, hb=hb):
                                v.wait_ge(sA.h, a_done)
                                for hc in hcs:
                                    sV.inc(v.scalar_tensor_tensor(
                                        ct2[:, hc % 8, :],
                                        hb[:, hc, c0 + 1:c0 + 129],
                                        cwq[:, 3 * hc + 1:3 * hc + 2],
                                        ct1[:, hc % 8, :], ALU.mult, ALU.add))
                            self.v_done = sV.n
                            _ps = self.silu.get(bi - 1, 0)

                            @blk.vector
                            def _(v, hcs=hcs, _ps=_ps, c0=c0, hb=hb):
                                if _ps:
                                    v.wait_ge(sA.h, _ps)
                                for hc in hcs:
                                    sV.inc(v.scalar_tensor_tensor(
                                        ct3[:, hc % 8, :],
                                        hb[:, hc, c0:c0 + 128],
                                        cwq[:, 3 * hc:3 * hc + 1],
                                        ct2[:, hc % 8, :], ALU.mult, ALU.add))
                            self.g_done[bi] = sV.n

                        def emit_silu(self, bi):
                            if bi < 0 or bi in self.silu:
                                return
                            g_done = self.g_done[bi]
                            _uf = self.uf
                            hcs = range(8 * bi, 8 * bi + 8)

                            @blk.scalar
                            def _(s, hcs=hcs, g_done=g_done, _uf=_uf, bi=bi):
                                s.wait_ge(sV.h, g_done)
                                if bi == 0 and _uf:
                                    s.wait_ge(sV.h, _uf)
                                for hc in hcs:
                                    sA.inc(s.activation(u_sb[:, hc, :],
                                                        ct3[:, hc % 8, :], AF.Silu))
                            self.silu[bi] = sA.n

                        def finish(self):
                            self.emit_silu(3)
                            self.silu_done = self.silu[3]

                    def emit_stage(i):
                        sz = HB_SZ[i]
                        tiles = [(2 * i, 0, 128), (2 * i + 1, 1, 128)]
                        if sz == 258:
                            tiles.append((2 * i + 2, 2, 2))
                        _ytf = state["ynqT_free"]
                        ynq_done_tile = []
                        for ti, (jt, sl, cnt) in enumerate(tiles):
                            _sf = state["stg_free"]

                            @blk.sync
                            def _(q, jt=jt, cnt=cnt, _sf=_sf):
                                q.wait_ge(scc.h, 1)
                                q.wait_ge(sV.h, _sf)
                                sd.inc(q.dma_start(
                                    out=scrd[0:cnt, 0, :],
                                    in_=d_x0s[128 * jt:128 * jt + cnt, :]), 16)
                                sd.inc(q.dma_start(
                                    out=scrd[0:cnt, 1, :],
                                    in_=d_ccout[128 * jt:128 * jt + cnt, :]), 16)
                            dmac = sd.n

                            @blk.vector
                            def _(v, jt=jt, ti=ti, cnt=cnt, dmac=dmac):
                                v.wait_ge(sd.h, dmac)
                                v.tensor_add(scrd[0:cnt, 2, :], scrd[0:cnt, 0, :],
                                             scrd[0:cnt, 1, :])
                                v.tensor_scalar(scrd[0:cnt, 0, :], scrd[0:cnt, 2, :],
                                                m_arr[0:cnt, jt:jt + 1],
                                                rsx_arr[0:cnt, jt:jt + 1],
                                                ALU.subtract, ALU.mult)
                                v.tensor_scalar(scrd[0:cnt, 1, :], scrd[0:cnt, 0, :],
                                                MAGIC, MAGIC, ALU.add, ALU.subtract)
                                sV.inc(v.tensor_scalar_mul(ynq_tm[0:cnt, ti % 2, :],
                                                           scrd[0:cnt, 1, :],
                                                           isx_arr[0:cnt, jt:jt + 1]))
                            ynq_done_tile.append(sV.n)
                            state["stg_free"] = sV.n

                            _cp = state["pt2_copy"].get(ti % 2, 0)
                            _yd = ynq_done_tile[ti]

                            @blk.tensor
                            def _(t, ti=ti, cnt=cnt, _yd=_yd, _cp=_cp, _ytf=_ytf):
                                t.wait_ge(sV.h, _yd)
                                if _cp:
                                    t.wait_ge(sV.h, _cp)
                                if ti == 0 and _ytf:
                                    t.wait_ge(sT.h, _ytf)
                                mm = None
                                for dc in range(8):
                                    mm = t.transpose(
                                        p_t2[:, ti % 2, 128 * dc:128 * dc + cnt],
                                        ynq_tm[0:cnt, ti % 2, dc * 128:(dc + 1) * 128],
                                        eyeb[0:cnt, 0:cnt])
                                sT.inc(mm)
                            trb = sT.n

                            @blk.vector
                            def _(v, ti=ti, cnt=cnt, trb=trb):
                                v.wait_ge(sT.h, trb)
                                sV.inc(v.tensor_copy(
                                    ynqT[:, 0:8, 128 * ti:128 * ti + cnt],
                                    p_t2[:, ti % 2, :].rearrange(
                                        "p (a b) -> p a b", a=8)[:, :, 0:cnt]))
                            state["pt2_copy"][ti % 2] = sV.n
                        state["ynqT_done"] = sV.n

                    def emit_m1(i, ob):
                        sz = HB_SZ[i]
                        ce0 = None
                        if ob is not None:
                            hv, ha = state["hb_evac"][ob]
                            ce0 = ConvEmitter(ob, 0, hv, ha)
                        ynqT_done = state["ynqT_done"]
                        evac_at = [None] * 32
                        for hc in range(32):
                            _w = evac_at[hc - 2] if hc >= 2 else None
                            _cvd = state["hb_conv_done"].get(i - 2)

                            @blk.tensor
                            def _(t, hc=hc, sz=sz, ynqT_done=ynqT_done, _w=_w):
                                if hc == 0:
                                    t.wait_ge(sV.h, ynqT_done)
                                    t.wait_ge(sV.h, w2q_ready)
                                if _w is not None:
                                    t.wait_ge(sV.h if _w[0] == "V" else sA.h, _w[1])
                                mm = None
                                for dc in range(8):
                                    mm = t.matmul(p_m1[:, hc % 2, 0:sz],
                                                  w1q[:, dc, 128 * hc:128 * hc + 128],
                                                  ynqT[:, dc, 0:sz],
                                                  start=(dc == 0), stop=(dc == 7))
                                sT.inc(mm)
                            m1_done = sT.n

                            if hc % 2 == 0:
                                @blk.vector
                                def _(v, hc=hc, sz=sz, i=i, m1_done=m1_done, _cvd=_cvd):
                                    v.wait_ge(sT.h, m1_done)
                                    if hc == 0 and _cvd is not None:
                                        v.wait_ge(sA.h, _cvd[0])
                                    sV.inc(v.tensor_copy(hbs[i % 2][:, hc, 0:sz],
                                                         p_m1[:, hc % 2, 0:sz]))
                                    if i >= 1:
                                        sV.inc(v.tensor_copy(
                                            hbs[(i - 1) % 2][:, hc, 256:258],
                                            p_m1[:, hc % 2, 0:2]))
                                evac_at[hc] = ("V", sV.n)
                            else:
                                @blk.scalar
                                def _(s, hc=hc, sz=sz, i=i, m1_done=m1_done, _cvd=_cvd):
                                    s.wait_ge(sT.h, m1_done)
                                    if hc == 1 and _cvd is not None:
                                        s.wait_ge(sA.h, _cvd[0])
                                    sA.inc(s.activation(hbs[i % 2][:, hc, 0:sz],
                                                        p_m1[:, hc % 2, 0:sz], AF.Copy))
                                    if i >= 1:
                                        sA.inc(s.activation(
                                            hbs[(i - 1) % 2][:, hc, 256:258],
                                            p_m1[:, hc % 2, 0:2], AF.Copy))
                                evac_at[hc] = ("A", sA.n)

                            if ce0 is not None and hc in (7, 15, 23, 31):
                                ce0.emit_batch(hc // 8)
                        if ce0 is not None:
                            ce0.finish()
                        state["hb_evac"][i] = (sV.n, sA.n)
                        if i >= 1:
                            state["halo_done"][i - 1] = (sV.n, sA.n)
                        state["ynqT_free"] = sT.n
                        return ce0.silu_done if ce0 is not None else None

                    def emit_sb_tail(ob, sb, silu_done, res_dma):
                        @blk.vector
                        def _(v, silu_done=silu_done):
                            v.wait_ge(sA.h, silu_done)
                            sV.inc(v.tensor_reduce(
                                mm1s[:, :],
                                u_sb.ap().rearrange("p c j -> p j c"),
                                AX.X, ALU.max, apply_absolute_value=True))
                        red_done = sV.n

                        @blk.tensor
                        def _(t, red_done=red_done):
                            t.wait_ge(sV.h, red_done)
                            sT.inc(t.transpose(p_sh[:, 0:128], mm1s[:, :],
                                               eyef[:, :]))
                        tr_done = sT.n

                        @blk.vector
                        def _(v, tr_done=tr_done, sb=sb):
                            v.wait_ge(sT.h, tr_done)
                            def VF(instr):
                                sV.inc(instr)
                                v.wait_ge(sV.h, sV.n)
                            VF(v.reduce_max(amu[:, sb:sb + 1], p_sh[:, 0:128], AX.X))
                            VF(v.tensor_scalar_max(shv[:, sb:sb + 1],
                                                   amu[:, sb:sb + 1], 1e-5))
                            VF(v.tensor_single_scalar(beta[:, sb:sb + 1],
                                                      shv[:, sb:sb + 1],
                                                      r127sw2[:, 0:1], ALU.mult))
                            VF(v.reciprocal(amu[:, sb:sb + 1], shv[:, sb:sb + 1]))
                            VF(v.tensor_scalar_mul(shv[:, sb:sb + 1],
                                                   amu[:, sb:sb + 1], 127.0))
                            VF(v.tensor_copy(fsc[:, 0:1], shv[:, sb:sb + 1]))
                            VF(v.tensor_copy(fsc[:, 1:2], fsc[:, 0:1]))
                        sh_done = sV.n

                        @blk.tensor
                        def _(t, sh_done=sh_done, sb=sb):
                            t.wait_ge(sV.h, sh_done)
                            sT.inc(t.transpose(p_sh[0:1, 128:256],
                                               shv[:, sb:sb + 1], eyef[:, :]))
                        shr_done = sT.n

                        @blk.vector
                        def _(v, shr_done=shr_done):
                            v.wait_ge(sT.h, shr_done)
                            sV.inc(v.tensor_copy(shrow[0:1, :], p_sh[0:1, 128:256]))
                            v.wait_ge(sV.h, sV.n)
                            sV.inc(v.tensor_copy(fsc[0:1, 0:8], shrow[0:1, 0:8]))
                            v.wait_ge(sV.h, sV.n)
                            sV.inc(v.tensor_copy(fsc[0:1, 8:16], shrow[0:1, 8:16]))
                        shc_done = sV.n

                        @blk.tensor
                        def _(t, shc_done=shc_done):
                            t.wait_ge(sV.h, shc_done)
                            sT.inc(t.matmul(p_sh[:, 256:384], onesf[0:1, :],
                                            shrow[0:1, :], start=True, stop=True))
                        shb_done = sT.n

                        @blk.vector
                        def _(v, shb_done=shb_done):
                            v.wait_ge(sT.h, shb_done)
                            sV.inc(v.tensor_copy(shbc[:, :], p_sh[:, 256:384]))
                        shbc_done = sV.n

                        _uqf = state["uq_free"]
                        for bi in range(4):
                            hcs = range(8 * bi, 8 * bi + 8)

                            @blk.vector
                            def _(v, hcs=hcs, shbc_done=shbc_done, _uqf=_uqf, bi=bi):
                                if bi == 0:
                                    v.wait_ge(sV.h, shbc_done)
                                    if _uqf:
                                        v.wait_ge(sT.h, _uqf)
                                for hc in hcs:
                                    v.tensor_mul(ct1[:, hc % 8, :],
                                                 u_sb[:, hc, :], shbc[:, :])
                                for hc in hcs:
                                    sV.inc(v.tensor_scalar(uq[:, hc, :],
                                                           ct1[:, hc % 8, :],
                                                           MAGIC, MAGIC,
                                                           ALU.add, ALU.subtract))
                        uq_done = sV.n
                        state["u_free"] = sV.n
                        state["round_done"] = sV.n

                        _pyf = state["py_free"]

                        @blk.tensor
                        def _(t, uq_done=uq_done, _pyf=_pyf):
                            t.wait_ge(sV.h, uq_done)
                            if _pyf:
                                t.wait_ge(sV.h, _pyf)
                            for n in range(2):
                                mm = None
                                for hc in range(32):
                                    mm = t.matmul(
                                        p_y[:, 512 * n:512 * n + 512],
                                        uq[:, hc, :],
                                        w2q[:, hc, 512 * n:512 * n + 512],
                                        start=(hc == 0), stop=(hc == 31))
                                sT.inc(mm)
                        y_done = sT.n
                        state["uq_free"] = sT.n

                        _od = state["out_dma"]

                        @blk.vector
                        def _(v, sb=sb, y_done=y_done, res_dma=res_dma, _od=_od):
                            v.wait_ge(sT.h, y_done)
                            v.wait_ge(sd.h, res_dma)
                            if _od:
                                v.wait_ge(sd.h, _od)
                            ct2f = ct2.ap().rearrange("p a b -> p (a b)")
                            v.scalar_tensor_tensor(
                                ct2f, p_y[:, :],
                                beta[:, sb:sb + 1], xres[:, sb, :],
                                ALU.mult, ALU.add)
                            sV.inc(v.tensor_add(outsb[:, sb, :], ct2f,
                                                ares[:, sb, :]))
                        state["py_free"] = sV.n

                    def emit_out_block(ob, woven_silu=None):
                        hd = state["halo_done"].get(ob)
                        if hd is None:
                            hd = state["hb_evac"][ob]

                        _rf = state["res_free"]

                        @blk.sync
                        def _(q, ob=ob, _rf=_rf):
                            q.wait_ge(sV.h, _rf)
                            r0 = 256 * ob + 1
                            sd.inc(q.dma_start(
                                out=xres[:, :, :],
                                in_=d_x0s[r0:r0 + 256, :].rearrange(
                                    "(tc p) d -> p tc d", p=128)), 16)
                            sd.inc(q.dma_start(
                                out=ares[:, :, :],
                                in_=d_ccout[r0:r0 + 256, :].rearrange(
                                    "(tc p) d -> p tc d", p=128)), 16)
                        res_dma = sd.n

                        if woven_silu is None:
                            hv, ha = state["hb_evac"][ob]
                            ce0 = ConvEmitter(ob, 0, hv, ha)
                            for bi in range(4):
                                ce0.emit_batch(bi)
                            ce0.finish()
                            silu0 = ce0.silu_done
                        else:
                            silu0 = woven_silu
                        emit_sb_tail(ob, 0, silu0, res_dma)

                        ce1 = ConvEmitter(ob, 1, hd[0], hd[1])
                        for bi in range(4):
                            ce1.emit_batch(bi)
                        ce1.finish()
                        emit_sb_tail(ob, 1, ce1.silu_done, res_dma)

                        ev_done = sV.n
                        state["res_free"] = sV.n

                        @blk.sync
                        def _(q, ob=ob, ev_done=ev_done):
                            q.wait_ge(sV.h, ev_done)
                            sd.inc(q.dma_start(
                                out=d_out[256 * ob:256 * ob + 256, :].rearrange(
                                    "(tc p) d -> p tc d", p=128),
                                in_=outsb[:, :, :]), 16)
                        state["out_dma"] = sd.n

                        state["hb_conv_done"][ob] = (sA.n, sV.n, sG.n)

                    emit_stage(0)
                    emit_m1(0, None)
                    for i in range(1, 8):
                        emit_stage(i)
                        ws = emit_m1(i, i - 1)
                        emit_out_block(i - 1, woven_silu=ws)
                    emit_out_block(7)

                    t_w1 = tapt("w1q", (128, 8 * HID))
                    t_w2 = tapt("w2q", (128, 32 * DM))
                    t_st = tapt("dstats", (128, 96))
                    if t_w1 is not None:
                        for part in range(8):
                            @blk.vector
                            def _(v, part=part):
                                v.wait_ge(sd.h, state["out_dma"])
                                sV.inc(v.tensor_copy(
                                    xres.ap().rearrange("p a b -> p (a b)")[:, 0:2048]
                                    if False else xres.ap().rearrange("p a b -> p (a b)")[:, 0:2048],
                                    w1q.ap().rearrange("p a b -> p (a b)")[
                                        :, 4096 * part:4096 * part + 2048]))
                                sV.inc(v.tensor_copy(
                                    ares.ap().rearrange("p a b -> p (a b)")[:, 0:2048],
                                    w1q.ap().rearrange("p a b -> p (a b)")[
                                        :, 4096 * part + 2048:4096 * part + 4096]))
                            vtap = sV.n

                            @blk.sync
                            def _(q, part=part, vtap=vtap):
                                q.wait_ge(sV.h, vtap)
                                sd.inc(q.dma_start(
                                    out=t_w1[:, 4096 * part:4096 * part + 2048],
                                    in_=xres.ap().rearrange("p a b -> p (a b)")[:, 0:2048]), 16)
                                sd.inc(q.dma_start(
                                    out=t_w1[:, 4096 * part + 2048:4096 * part + 4096],
                                    in_=ares.ap().rearrange("p a b -> p (a b)")[:, 0:2048]), 16)

                            @blk.vector
                            def _(v):
                                v.wait_ge(sd.h, sd.n)
                    if t_w2 is not None:
                        for part in range(8):
                            @blk.vector
                            def _(v, part=part):
                                v.wait_ge(sd.h, sd.n)
                                sV.inc(v.tensor_copy(
                                    xres.ap().rearrange("p a b -> p (a b)")[:, 0:2048],
                                    w2q.ap().rearrange("p a b -> p (a b)")[
                                        :, 4096 * part:4096 * part + 2048]))
                                sV.inc(v.tensor_copy(
                                    ares.ap().rearrange("p a b -> p (a b)")[:, 0:2048],
                                    w2q.ap().rearrange("p a b -> p (a b)")[
                                        :, 4096 * part + 2048:4096 * part + 4096]))
                            vtap = sV.n

                            @blk.sync
                            def _(q, part=part, vtap=vtap):
                                q.wait_ge(sV.h, vtap)
                                sd.inc(q.dma_start(
                                    out=t_w2[:, 4096 * part:4096 * part + 2048],
                                    in_=xres.ap().rearrange("p a b -> p (a b)")[:, 0:2048]), 16)
                                sd.inc(q.dma_start(
                                    out=t_w2[:, 4096 * part + 2048:4096 * part + 4096],
                                    in_=ares.ap().rearrange("p a b -> p (a b)")[:, 0:2048]), 16)

                            @blk.vector
                            def _(v):
                                v.wait_ge(sd.h, sd.n)
                    if t_st is not None:
                        @blk.vector
                        def _(v):
                            v.wait_ge(sd.h, sd.n)
                            v.tensor_copy(outsb[:, 0, 0:17], m_arr[:, :])
                            v.tensor_copy(outsb[:, 0, 17:34], rsx_arr[:, :])
                            v.tensor_copy(outsb[:, 0, 34:51], isx_arr[:, :])
                            v.tensor_copy(outsb[:, 0, 51:68], am_arr[:, :])
                            v.tensor_copy(outsb[:, 0, 68:69], s1bc[:, :])
                            v.tensor_copy(outsb[:, 0, 69:70], s2bc[:, :])
                            v.tensor_copy(outsb[:, 0, 70:71], r127sw2[:, :])
                            v.tensor_copy(outsb[:, 0, 71:73], beta[:, :])
                            v.tensor_copy(outsb[:, 0, 73:75], shv[:, :])
                            v.tensor_copy(outsb[:, 0, 75:78], scal[:, 0:3])
                            sV.inc(v.tensor_copy(outsb[:, 0, 80:88], sttd[:, 0:8]))
                        vtap = sV.n

                        @blk.sync
                        def _(q, vtap=vtap):
                            q.wait_ge(sV.h, vtap)
                            sd.inc(q.dma_start(out=t_st[:, 0:96],
                                               in_=outsb[:, 0, 0:96]), 16)

                    @blk.sync
                    def _(q):
                        q.wait_ge(sd.h, sd.n)

    return nc, tapd



def kernel(L0_attn_x, L0_attn_w_qkv, L0_attn_w_out, L0_ffn_w1, L0_ffn_conv_w,
           L0_ffn_conv_b, L0_ffn_w2):
    x = np.asarray(L0_attn_x, dtype=np.float32)
    in_maps = _stage_inputs(x,
                            np.asarray(L0_attn_w_qkv, dtype=np.float32),
                            np.asarray(L0_attn_w_out, dtype=np.float32),
                            np.asarray(L0_ffn_w1, dtype=np.float32),
                            np.asarray(L0_ffn_conv_w, dtype=np.float32),
                            np.asarray(L0_ffn_conv_b, dtype=np.float32),
                            np.asarray(L0_ffn_w2, dtype=np.float32))
    nc, _ = build_program()
    res = run_bass_kernel_spmd(nc, in_maps, core_ids=list(range(NCORE)))
    out = np.empty((4, S, DM), dtype=np.float32)
    for core in range(NCORE):
        b, hh = core // 2, core % 2
        out[b, hh * OWN:(hh + 1) * OWN] = res.results[core]["out"]
    return out
